# version 07
# baseline (speedup 1.0000x reference)
import numpy as np
import ml_dtypes
from contextlib import ExitStack

import concourse.bass as bass
import concourse.mybir as mybir
from concourse.bass_utils import run_bass_kernel_spmd

F32 = mybir.dt.float32
BF16 = mybir.dt.bfloat16
I8 = mybir.dt.int8
I32 = mybir.dt.int32
AF = mybir.ActivationFunctionType
ALU = mybir.AluOpType
BF = ml_dtypes.bfloat16

NCORES = 8
T = 2048
NTC = T // 128
DM = 2048
NDC = DM // 128
FFN0_I = 8192
FFN1_I = 5632
EPSLN = 1e-5
EPS = 1e-8



class Prog:
    def __init__(self):
        self.nc = bass.Bass("TRN2", target_bir_lowering=False)
        self.es = ExitStack()
        self._sems = {}
        self.cnt = {}
        self.bar_k = 0

    def sem(self, name):
        if name not in self._sems:
            self._sems[name] = self.es.enter_context(self.nc.semaphore(name))
            self.cnt[name] = 0
        return self._sems[name]

    def inc(self, instr, name, k=1):
        s = self.sem(name)
        instr.then_inc(s, k)
        self.cnt[name] += k
        return self.cnt[name]

    def n(self, name):
        self.sem(name)
        return self.cnt[name]

    def sb(self, name, shape, dtype, stack=None):
        return (stack or self.es).enter_context(self.nc.sbuf_tensor(name, list(shape), dtype))

    def ps(self, name, shape, dtype=F32, stack=None):
        return (stack or self.es).enter_context(self.nc.psum_tensor(name, list(shape), dtype))

    def dram(self, name, shape, dtype, kind="Internal"):
        return self.nc.dram_tensor(name, list(shape), dtype, kind=kind)

    def barrier(self):
        bar = self.sem("bar")
        self.bar_k += 1
        tgt = 5 * self.bar_k
        with self.nc.Block() as blk:
            for engname in ("tensor", "vector", "scalar", "gpsimd", "sync"):
                def mk(e, _t=tgt):
                    e.drain().then_inc(bar, 1)
                    e.wait_ge(bar, _t)
                getattr(blk, engname)(mk)
        self.cnt["bar"] = tgt

    def finish(self):
        self.es.close()
        return self.nc

    def edge(self, e, instr, nm):
        c = self.inc(instr, nm)
        e.wait_ge(self.sem(nm), c)

    def dq(self, name, depth=2):
        key = (name, depth)
        if not hasattr(self, "_dqs"):
            self._dqs = {}
        if key not in self._dqs:
            self._dqs[key] = DQ(self, name, depth)
        return self._dqs[key]


class DQ:

    def __init__(self, p, name, depth):
        self.p, self.name, self.depth = p, name, depth
        self.k = 0

    def issue(self, e, out, in_, **kw):
        s = f"{self.name}{self.k % self.depth}"
        self.p.inc(e.dma_start(out=out, in_=in_, **kw), s, 16)
        self.k += 1
        return self.k

    def wait(self, e, ticket):
        i = ticket - 1
        s = f"{self.name}{i % self.depth}"
        e.wait_ge(self.p.sem(s), 16 * (i // self.depth + 1))

    def wait_all(self, e):
        for d in range(self.depth):
            n = (self.k + self.depth - 1 - d) // self.depth
            if n:
                e.wait_ge(self.p.sem(f"{self.name}{d}"), 16 * n)



def emit_const_setup(p, g):
    nc = p.nc
    g["ident_bf"] = p.sb("ident_bf", (128, 128), BF16)
    g["ident_f32"] = p.sb("ident_f32", (128, 128), F32)
    g["ones_bf"] = p.sb("ones_bf", (1, 512), BF16)
    g["epsln_col"] = p.sb("epsln_col", (128, 1), F32)
    g["neg1_col"] = p.sb("neg1_col", (128, 1), F32)
    g["ones_col"] = p.sb("ones_col", (128, 1), BF16)
    g["ones_row_f32"] = p.sb("ones_row_f32", (1, 128), F32)
    g["qscale_col"] = p.sb("qscale_col", (128, 1), F32)
    cst = p.dq("s2", 4)
    with nc.Block() as blk:
        @blk.sync
        def _(e):
            cst.issue(e, g["ident_bf"][:, :], g["d_ident"][:, :])
            cst.issue(e, g["ident_f32"][:, :], g["d_ident_f32"][:, :])
            cst.wait_all(e)

        @blk.vector
        def _(e):
            p.inc(e.memset(g["ones_bf"][:, :], 1.0), "ve", 1)
            p.inc(e.memset(g["epsln_col"][:, :], EPSLN), "ve", 1)
            p.inc(e.memset(g["neg1_col"][:, :], -1.0), "ve", 1)
            p.inc(e.memset(g["ones_col"][:, :], 1.0), "ve", 1)
            p.inc(e.memset(g["ones_row_f32"][:, :], 1.0), "ve", 1)
            p.inc(e.memset(g["qscale_col"][:, :], 0.08838834764831845), "ve", 1)
    p.barrier()



def emit_ln(p, x_dram, xnT, g, tag):
    nc = p.nc
    st = ExitStack()
    xdt = F32
    xt = [p.sb(f"{tag}_xt{i}", (128, DM), xdt, st) for i in range(2)]
    xn = [p.sb(f"{tag}_xn{i}", (128, DM), BF16, st) for i in range(2)]
    st6 = p.sb(f"{tag}_st6", (128, 24), F32, st)
    mv = p.sb(f"{tag}_mv", (128, 2), F32, st)
    sq = p.sb(f"{tag}_sq", (128, 1), F32, st)
    rstd = p.sb(f"{tag}_rstd", (128, 1), F32, st)
    b2 = p.sb(f"{tag}_b2", (128, 1), F32, st)
    pT = [p.ps(f"{tag}_pT{i}", (128, 128), BF16, st) for i in range(2)]

    vs0 = p.n("vs")
    se0 = p.n("se")
    ve0 = p.n("ve")
    ev0 = p.n("ev")
    pe0 = p.n("pe")
    lnx = p.dq("s1", 4)
    tk0 = lnx.k

    with nc.Block() as blk:
        @blk.sync
        def _(e):
            for i in range(NTC):
                if i >= 2:
                    e.wait_ge(p.sem("ve"), ve0 + i - 1)
                lnx.issue(e, xt[i % 2][:, :], x_dram[i * 128:(i + 1) * 128, :])

        @blk.scalar
        def _(e):
            for i in range(NTC):
                e.wait_ge(p.sem("vs"), vs0 + i + 1)
                p.inc(e.activation(sq[:, :], mv[:, 1:2], AF.Sqrt, bias=g["epsln_col"][:, 0:1]), "se", 1)

        @blk.vector
        def _(e):
            for i in range(NTC):
                x = xt[i % 2]
                lnx.wait(e, tk0 + i + 1)
                for j in range(4):
                    e.bn_stats(st6[:, 6 * j:6 * j + 6], x[:, 512 * j:512 * (j + 1)])
                p.edge(e, e.drain(), "vch")
                p.inc(e.bn_aggr(mv[:, :], st6[:, :]), "vs", 1)
                e.wait_ge(p.sem("vs"), vs0 + i + 1)
                e.wait_ge(p.sem("se"), se0 + i + 1)
                r = e.reciprocal(rstd[:, :], sq[:, :])
                p.edge(e, r, "vch")
                b = e.scalar_tensor_tensor(b2[:, :], mv[:, 0:1], -1.0, rstd[:, :], ALU.mult, ALU.mult)
                p.edge(e, b, "vch")
                if i >= 2:
                    e.wait_ge(p.sem("pe"), pe0 + (i - 1) * NDC)
                p.inc(e.tensor_scalar(xn[i % 2][:, :], x[:, :], rstd[:, 0:1], b2[:, 0:1],
                                      ALU.mult, ALU.add), "ve", 1)
                for dc in range(NDC):
                    k = i * NDC + dc
                    e.wait_ge(p.sem("pe"), pe0 + k + 1)
                    p.inc(e.tensor_copy(xnT[:, dc * T + i * 128:dc * T + (i + 1) * 128],
                                        pT[k % 2][:, :]), "ev", 1)

        @blk.tensor
        def _(e):
            for i in range(NTC):
                e.wait_ge(p.sem("ve"), ve0 + i + 1)
                for dc in range(NDC):
                    k = i * NDC + dc
                    if k >= 2:
                        e.wait_ge(p.sem("ev"), ev0 + k - 1)
                    p.inc(e.transpose(pT[k % 2][:, :], xn[i % 2][:, dc * 128:(dc + 1) * 128],
                                      g["ident_bf"][:, :]), "pe", 1)
    st.close()
    p.barrier()



def emit_ffn0(p, xnT, g, g0_dram):
    nc = p.nc
    NIC = FFN0_I // 128
    st = ExitStack()
    wg = [p.sb(f"f0_wg{i}", (128, 2 * DM), BF16, st) for i in range(2)]
    wu = [p.sb(f"f0_wu{i}", (128, 2 * DM), BF16, st) for i in range(2)]
    wd = [p.sb(f"f0_wd{i}", (128, 2048), BF16, st) for i in range(2)]
    sg = p.sb("f0_sg", (128, NIC), F32, st)
    su = p.sb("f0_su", (128, NIC), F32, st)
    sdb = p.sb("f0_sdb", (128, DM), F32, st)
    fuse = p.sb("f0_fuse", (128, NIC * 512), BF16, st)
    gsc = [p.sb(f"f0_gsc{i}", (128, 512), BF16, st) for i in range(2)]
    usc = [p.sb(f"f0_usc{i}", (128, 512), BF16, st) for i in range(2)]
    ysc = [p.sb(f"f0_ysc{i}", (128, 512), F32, st) for i in range(2)]
    g0t = [p.sb(f"f0_g0t{i}", (128, 512), BF16, st) for i in range(2)]

    sc0 = p.dq("s2", 4)
    with nc.Block() as blk:
        @blk.sync
        def _(e):
            sc0.issue(e, sg[:, :], g["d_sg_sw"][:, :])
            sc0.issue(e, su[:, :], g["d_su_sw"][:, :])
            sc0.issue(e, sdb[:, :], g["d_sdown_bc"][:, :])
            sc0.wait_all(e)
    p.barrier()

    for tq in range(4):
        t0 = tq * 512
        stq = ExitStack()
        pg = [p.ps(f"f0_pg{tq}_{i}", (128, 512), F32, stq) for i in range(2)]
        pu = [p.ps(f"f0_pu{tq}_{i}", (128, 512), F32, stq) for i in range(2)]
        se0 = p.n("se")
        ve0 = p.n("ve")
        pe0 = p.n("pe")
        wq = p.dq("g1", 8)
        wtk0 = wq.k
        with nc.Block() as blk:
            @blk.gpsimd
            def _(e):
                for j in range(NIC // 2):
                    if j >= 2:
                        e.wait_ge(p.sem("pe"), pe0 + 4 * (j - 1))
                    wq.issue(e, wg[j % 2][:, :].rearrange("p (i o) -> p i o", i=2),
                             g["d_wg_sw"][2 * j:2 * j + 2].rearrange("i p o -> p i o"))
                    wq.issue(e, wu[j % 2][:, :].rearrange("p (i o) -> p i o", i=2),
                             g["d_wu_sw"][2 * j:2 * j + 2].rearrange("i p o -> p i o"))

            @blk.tensor
            def _(e):
                for ic in range(NIC):
                    j, jo = ic // 2, (ic % 2) * DM
                    wq.wait(e, wtk0 + 2 * j + 1)
                    wq.wait(e, wtk0 + 2 * j + 2)
                    if ic >= 2:
                        e.wait_ge(p.sem("se"), se0 + 2 * (ic - 1))
                    for dc in range(NDC):
                        mm = e.matmul(pg[ic % 2][:, :], wg[j % 2][:, jo + dc * 128:jo + (dc + 1) * 128],
                                      xnT[:, dc * T + t0:dc * T + t0 + 512],
                                      start=(dc == 0), stop=(dc == NDC - 1))
                    p.inc(mm, "pe", 1)
                    for dc in range(NDC):
                        mm = e.matmul(pu[ic % 2][:, :], wu[j % 2][:, jo + dc * 128:jo + (dc + 1) * 128],
                                      xnT[:, dc * T + t0:dc * T + t0 + 512],
                                      start=(dc == 0), stop=(dc == NDC - 1))
                    p.inc(mm, "pe", 1)

            @blk.scalar
            def _(e):
                for ic in range(NIC):
                    e.wait_ge(p.sem("pe"), pe0 + 2 * ic + 1)
                    if ic >= 2:
                        e.wait_ge(p.sem("ve"), ve0 + ic - 1)
                    p.inc(e.activation(gsc[ic % 2][:, :], pg[ic % 2][:, :], AF.Silu,
                                       scale=sg[:, ic:ic + 1]), "se", 1)
                    e.wait_ge(p.sem("pe"), pe0 + 2 * ic + 2)
                    p.inc(e.activation(usc[ic % 2][:, :], pu[ic % 2][:, :], AF.Copy,
                                       scale=su[:, ic:ic + 1]), "se", 1)

            @blk.vector
            def _(e):
                for ic in range(NIC):
                    e.wait_ge(p.sem("se"), se0 + 2 * (ic + 1))
                    p.inc(e.tensor_mul(fuse[:, ic * 512:(ic + 1) * 512],
                                       gsc[ic % 2][:, :], usc[ic % 2][:, :]), "ve", 1)
        stq.close()
        p.barrier()

        for oh in range(2):
            std = ExitStack()
            pd = [[p.ps(f"f0_pd{tq}_{oh}_{t4}_{ob}", (128, 512), F32, std)
                   for ob in range(2)] for t4 in range(4)]
            pe0 = p.n("pe")
            ve0 = p.n("ve")
            se0 = p.n("se")
            wdq = p.dq("g1", 8)
            dtk0 = wdq.k
            gsq = p.dq("s3", 4)
            stk0 = gsq.k
            with nc.Block() as blk:
                @blk.gpsimd
                def _(e):
                    for j in range(NIC // 2):
                        if j >= 2:
                            e.wait_ge(p.sem("pe"), pe0 + 2 * (j - 1))
                        wdq.issue(e, wd[j % 2][:, :].rearrange("p (i o) -> p i o", i=2),
                                 g["d_wd_sw"][oh, 2 * j:2 * j + 2].rearrange("i p o -> p i o"))

                @blk.tensor
                def _(e):
                    for ic in range(NIC):
                        j, jo = ic // 2, (ic % 2) * 1024
                        wdq.wait(e, dtk0 + j + 1)
                        for t4 in range(4):
                            for ob in range(2):
                                mm = e.matmul(pd[t4][ob][:, :],
                                              fuse[:, ic * 512 + t4 * 128:ic * 512 + (t4 + 1) * 128],
                                              wd[j % 2][:, jo + ob * 512:jo + (ob + 1) * 512],
                                              start=(ic == 0), stop=(ic == NIC - 1))
                        p.inc(mm, "pe", 1)

                @blk.vector
                def _(e):
                    e.wait_ge(p.sem("pe"), pe0 + NIC)
                    for j in range(8):
                        t4, ob = j // 2, j % 2
                        ocol = oh * 1024 + ob * 512
                        if j >= 2:
                            e.wait_ge(p.sem("se"), se0 + j - 1)
                        p.inc(e.tensor_mul(ysc[j % 2][:, :], pd[t4][ob][:, :],
                                           sdb[:, ocol:ocol + 512]), "ve", 1)

                @blk.scalar
                def _(e):
                    for j in range(8):
                        e.wait_ge(p.sem("ve"), ve0 + j + 1)
                        if stk0 + j - 1 >= 1:
                            gsq.wait(e, stk0 + j - 1)
                        p.inc(e.activation(g0t[j % 2][:, :], ysc[j % 2][:, :], AF.Silu), "se", 1)

                @blk.sync
                def _(e):
                    for j in range(8):
                        t4, ob = j // 2, j % 2
                        ocol = oh * 1024 + ob * 512
                        e.wait_ge(p.sem("se"), se0 + j + 1)
                        gsq.issue(e, g0_dram[t0 + t4 * 128:t0 + (t4 + 1) * 128,
                                             ocol:ocol + 512], g0t[j % 2][:, :])
            std.close()
            p.barrier()
    with nc.Block() as blk:
        @blk.sync
        def _(e):
            p.dq("s3", 4).wait_all(e)
    p.barrier()
    st.close()



def emit_dump_xnT(p, xnT, dram, tag="dxnT"):
    q = p.dq("s1", 4)
    with p.nc.Block() as blk:
        @blk.sync
        def _(e):
            for dc in range(NDC):
                q.issue(e, dram[dc], xnT[:, dc * T:(dc + 1) * T])
            q.wait_all(e)
    p.barrier()


def emit_load_xnT(p, xnT, dram, tag="lxnT"):
    q = p.dq("s1", 4)
    with p.nc.Block() as blk:
        @blk.sync
        def _(e):
            for dc in range(NDC):
                q.issue(e, xnT[:, dc * T:(dc + 1) * T], dram[dc])
            q.wait_all(e)
    p.barrier()



def hostprep(inputs):
    h = {}
    C = np.ascontiguousarray
    h["WqT"] = C(inputs["L0_attn_Wq"].T)
    h["WkT"] = C(inputs["L0_attn_Wk"].T)
    h["WvT"] = C(inputs["L0_attn_Wv"].T)
    h["WgT"] = C(inputs["L0_attn_Wg"].T)
    h["WmbT"] = C(inputs["L0_attn_Wmb"].T)
    for nm in ("bq", "bk", "bv", "bg", "bmb"):
        b = inputs[f"L0_attn_{nm}"]
        h[f"{nm}_row"] = C(b.reshape(1, -1))
        h[f"{nm}_col"] = C(b.reshape(-1, 1))
    mem = inputs["L0_attn_memory"]
    nrm = inputs["L0_attn_norm"]
    h["mem_aug"] = C(np.concatenate([mem, nrm[:, :, None]], axis=2))
    h["WkT16"] = C(h["WkT"].reshape(16, 128, 512))
    h["WvT16"] = C(h["WvT"].reshape(16, 128, 512))
    h["wmb_hsw"] = C(h["WmbT"].reshape(16, 128, 16, 128).transpose(2, 0, 1, 3))
    h["wq_hsw"] = C(h["WqT"].reshape(16, 128, 16, 128).transpose(2, 0, 1, 3))
    h["wg_gsw"] = C(h["WgT"].reshape(16, 128, 4, 512).transpose(2, 0, 1, 3))
    h["bq_cols"] = C(inputs["L0_attn_bq"].reshape(16, 128).T)
    wg8 = inputs["L0_ffn_w_gate"]
    wu8 = inputs["L0_ffn_w_up"]
    wd8 = inputs["L0_ffn_w_down"]
    h["wg0_sw"] = C(wg8.reshape(16, 128, 64, 128).transpose(2, 1, 0, 3).reshape(64, 128, 2048))
    h["wu0_sw"] = C(wu8.reshape(16, 128, 64, 128).transpose(2, 1, 0, 3).reshape(64, 128, 2048))
    h["wd0_sw"] = C(wd8.reshape(64, 128, 2, 1024).transpose(2, 0, 1, 3))
    h["sg0_sw"] = C(inputs["L0_ffn_s_gate"].reshape(64, 128).T)
    h["su0_sw"] = C(inputs["L0_ffn_s_up"].reshape(64, 128).T)
    h["sd0_bc"] = C(np.broadcast_to(inputs["L0_ffn_s_down"].reshape(1, 2048), (128, 2048)))
    h["Wqkv"] = C(inputs["L1_attn_W_qkv"])
    bqkv = inputs["L1_attn_b_qkv"]
    h["bqkv_col"] = C(bqkv.reshape(-1, 1))
    h["bqkv_row"] = C(bqkv.reshape(1, -1))
    h["Wproj"] = C(inputs["L1_attn_W_proj"])
    h["bproj_col"] = C(inputs["L1_attn_b_proj"].reshape(-1, 1))
    Wq1 = inputs["L1_attn_W_qkv"][:, 0:2048]
    Wk1 = inputs["L1_attn_W_qkv"][:, 2048:4096]
    Wv1 = inputs["L1_attn_W_qkv"][:, 4096:6144]
    h["wq1_hsw"] = C(Wq1.reshape(16, 128, 16, 128).transpose(2, 0, 1, 3))
    h["wk1_hsw"] = C(Wk1.reshape(16, 128, 16, 128).transpose(2, 0, 1, 3))
    h["wv1_sw"] = C(Wv1.reshape(16, 128, 2048))
    h["bq1_rows"] = C(bqkv[0:2048].reshape(1, 2048))
    h["bk1_rows"] = C(bqkv[2048:4096].reshape(1, 2048))
    h["bv1_row"] = C(bqkv[4096:6144].reshape(1, 2048))
    h["wp_sw"] = C(inputs["L1_attn_W_proj"].reshape(16, 128, 16, 128).transpose(2, 0, 1, 3))
    h["bp_rows"] = C(inputs["L1_attn_b_proj"].reshape(1, 2048))
    wqg = inputs["L1_ffn_wq_gate"]
    wqu = inputs["L1_ffn_wq_up"]
    wqd = inputs["L1_ffn_wq_down"]
    wgT = wqg.T
    wuT = wqu.T
    h["wg1_sw"] = C(wgT.reshape(16, 128, 44, 128).transpose(2, 1, 0, 3).reshape(44, 128, 5632 * 0 + 2048))
    h["wu1_sw"] = C(wuT.reshape(16, 128, 44, 128).transpose(2, 1, 0, 3).reshape(44, 128, 2048))
    wdT = wqd.T
    h["wd1_sw"] = C(wdT.reshape(44, 128, 4, 512).transpose(2, 0, 1, 3))
    h["sg1_sw"] = C(inputs["L1_ffn_s_gate"].reshape(44, 128).T)
    h["su1_sw"] = C(inputs["L1_ffn_s_up"].reshape(44, 128).T)
    h["sd1_bc"] = C(np.broadcast_to(inputs["L1_ffn_s_down"].reshape(1, 2048), (128, 2048)))
    h["ident"] = np.eye(128, dtype=BF)
    h["ident_f32"] = np.eye(128, dtype=np.float32)
    x0 = inputs["L0_attn_hidden_states"]
    h["x0_cores"] = [C(x0[c // 2, (c % 2) * T:(c % 2 + 1) * T]) for c in range(NCORES)]
    return h



def emit_l0p1(p, xnT, g, ar_in, ar_out):
    nc = p.nc
    st = ExitStack()
    k_all = p.sb("l0_k", (128, NTC * 512), BF16, st)
    v_all = p.sb("l0_v", (128, NTC * 512), BF16, st)
    bk_sb = p.sb("l0_bk", (1, 512), BF16, st)
    bv_sb = p.sb("l0_bv", (1, 512), BF16, st)
    bmb_sb = p.sb("l0_bmb", (1, 2048), BF16, st)

    stA = ExitStack()
    wk_sb = p.sb("l0_wk", (128, NDC * 512), BF16, stA)
    wv_sb = p.sb("l0_wv", (128, NDC * 512), BF16, stA)
    pk = [p.ps(f"l0_pk{i}", (128, 512), F32, stA) for i in range(2)]
    pv = [p.ps(f"l0_pv{i}", (128, 512), F32, stA) for i in range(2)]
    wq = p.dq("g1", 8)
    pe0 = p.n("pe")
    se0 = p.n("se")
    with nc.Block() as blk:
        @blk.gpsimd
        def _(e):
            wq.issue(e, wk_sb[:, :].rearrange("p (dc o) -> p dc o", dc=NDC), g["d_WkT16"].rearrange("dc p o -> p dc o"))
            wq.issue(e, wv_sb[:, :].rearrange("p (dc o) -> p dc o", dc=NDC), g["d_WvT16"].rearrange("dc p o -> p dc o"))
            wq.issue(e, bk_sb[:, :], g["d_bk_row"][:, :])
            wq.issue(e, bv_sb[:, :], g["d_bv_row"][:, :])
            wq.issue(e, bmb_sb[:, :], g["d_bmb_row"][:, :])

        @blk.tensor
        def _(e):
            wq.wait_all(e)
            for tc in range(NTC):
                if tc >= 2:
                    e.wait_ge(p.sem("se"), se0 + 2 * (tc - 1))
                for dc in range(NDC):
                    e.matmul(pk[tc % 2][:, :], xnT[:, dc * T + tc * 128:dc * T + (tc + 1) * 128],
                             wk_sb[:, dc * 512:(dc + 1) * 512], start=(dc == 0), stop=False)
                mm = e.matmul(pk[tc % 2][:, :], g["ones_bf"][0:1, 0:128], bk_sb[:, :],
                              start=False, stop=True)
                p.inc(mm, "pe", 1)
                for dc in range(NDC):
                    e.matmul(pv[tc % 2][:, :], xnT[:, dc * T + tc * 128:dc * T + (tc + 1) * 128],
                             wv_sb[:, dc * 512:(dc + 1) * 512], start=(dc == 0), stop=False)
                mm = e.matmul(pv[tc % 2][:, :], g["ones_bf"][0:1, 0:128], bv_sb[:, :],
                              start=False, stop=True)
                p.inc(mm, "pe", 1)

        @blk.scalar
        def _(e):
            for tc in range(NTC):
                e.wait_ge(p.sem("pe"), pe0 + 2 * tc + 1)
                p.inc(e.activation(k_all[:, tc * 512:(tc + 1) * 512], pk[tc % 2][:, :],
                                   AF.Copy), "se", 1)
                e.wait_ge(p.sem("pe"), pe0 + 2 * tc + 2)
                p.inc(e.activation(v_all[:, tc * 512:(tc + 1) * 512], pv[tc % 2][:, :],
                                   AF.Copy), "se", 1)
    stA.close()
    p.barrier()

    mk_g = p.sb("l0_mk", (128, NTC * 768), BF16, st)
    mkT_g = p.sb("l0_mkT", (128, 6 * T), BF16, st)
    wmv_g = p.sb("l0_wmv", (128, NTC * 512), BF16, st)
    wmb_h = [p.sb(f"l0_wmbh{i}", (128, NDC * 128), BF16, st) for i in range(2)]
    mb_h = p.sb("l0_mbh", (128, NTC * 128), BF16, st)
    mem_sb = [p.sb(f"l0_mem{i}", (128, 6 * 129), BF16, st) for i in range(2)]
    x2 = p.sb("l0_x2", (128, 256), BF16, st)
    dsq = p.sb("l0_dsq", (128, 768), F32, st)
    ssq_g = p.sb("l0_ssq", (128, 16), F32, st)
    rn_g = p.sb("l0_rn", (128, 16), F32, st)
    mksq_g = p.sb("l0_mksq", (128, 16), F32, st)
    rmksq_g = p.sb("l0_rmksq", (128, 16), F32, st)
    den_g = p.sb("l0_den", (128, 64), F32, st)
    dsb2 = [p.sb(f"l0_dsb{i}", (128, 512), F32, st) for i in range(2)]
    coef_g = p.sb("l0_coef", (128, 64), BF16, st)
    rden_c = p.sb("l0_rden", (128, 1), F32, st)
    scr1 = p.sb("l0_scr1", (128, 128), F32, st)
    scr2 = p.sb("l0_scr2", (128, 128), F32, st)
    dnsb = p.sb("l0_dnsb", (4, 768), F32, st)
    coef_v = coef_g[:, :].rearrange("p (tc h) -> p tc h", h=4)

    memq = p.dq("g2", 4)
    wmbq = p.dq("g1", 8)
    arq = p.dq("s4", 8)
    dnq = p.dq("s2", 4)
    art0 = arq.k
    dnt0 = dnq.k

    for gi in range(4):
        stB = ExitStack()
        pT = [p.ps(f"l0_pT{gi}_{i}", (128, 128), BF16, stB) for i in range(2)]
        se0 = p.n("se")
        ve0 = p.n("ve")
        pe0 = p.n("pe")
        ev0 = p.n("ev")
        with nc.Block() as blk:
            @blk.scalar
            def _(e):
                for tc in range(NTC):
                    if tc >= 1:
                        e.wait_ge(p.sem("ve"), ve0 + 2 * tc)
                    p.inc(e.activation(x2[:, 0:128],
                                       k_all[:, tc * 512 + gi * 128:tc * 512 + (gi + 1) * 128],
                                       AF.Relu), "se", 1)
                    p.inc(e.activation(x2[:, 128:256],
                                       k_all[:, tc * 512 + gi * 128:tc * 512 + (gi + 1) * 128],
                                       AF.Relu, scale=g["neg1_col"][:, 0:1]), "se", 1)
                e.wait_ge(p.sem("ve"), ve0 + 2 * NTC)
                p.inc(e.activation(rn_g[:, :], ssq_g[:, :], AF.Sqrt), "se", 1)

            @blk.vector
            def _(e):
                for tc in range(NTC):
                    e.wait_ge(p.sem("se"), se0 + 2 * (tc + 1))
                    o = tc * 768
                    last = None
                    for j in range(1, 4):
                        b = (j - 1) * 256
                        last = e.tensor_mul(mk_g[:, o + b + j:o + b + 256], x2[:, j:256], x2[:, 0:256 - j])
                        last = e.tensor_mul(mk_g[:, o + b:o + b + j], x2[:, 0:j], x2[:, 256 - j:256])
                    p.edge(e, e.drain(), "vch")
                    p.inc(e.nop(), "ve", 1)
                    d = e.tensor_mul(dsq[:, :], mk_g[:, o:o + 768], mk_g[:, o:o + 768])
                    p.edge(e, d, "vch")
                    r = e.tensor_reduce(ssq_g[:, tc:tc + 1], dsq[:, :], mybir.AxisListType.X, ALU.add)
                    p.inc(r, "ve", 1)
                e.wait_ge(p.sem("se"), se0 + 2 * NTC + 1)
                r = e.tensor_scalar_max(rn_g[:, :], rn_g[:, :], 1e-12)
                p.edge(e, r, "vch")
                r = e.reciprocal(rn_g[:, :], rn_g[:, :])
                p.edge(e, r, "vch")
                r = e.tensor_mul(mksq_g[:, :], ssq_g[:, :], rn_g[:, :])
                p.edge(e, r, "vch")
                r = e.tensor_mul(mksq_g[:, :], mksq_g[:, :], rn_g[:, :])
                p.edge(e, r, "vch")
                r = e.reciprocal(rmksq_g[:, :], mksq_g[:, :])
                p.inc(r, "ve", 1)
                for tc in range(NTC):
                    r = e.tensor_scalar_mul(mk_g[:, tc * 768:(tc + 1) * 768],
                                            mk_g[:, tc * 768:(tc + 1) * 768], rn_g[:, tc:tc + 1])
                    p.inc(r, "ve", 1)
                for kk in range(NTC * 6):
                    e.wait_ge(p.sem("pe"), pe0 + kk + 1)
                    tcc, fc = kk // 6, kk % 6
                    p.inc(e.tensor_copy(mkT_g[:, fc * T + tcc * 128:fc * T + (tcc + 1) * 128],
                                        pT[kk % 2][:, :]), "ev", 1)

            @blk.tensor
            def _(e):
                for tc in range(NTC):
                    e.wait_ge(p.sem("ve"), ve0 + 2 * NTC + 1 + tc + 1)
                    for fc in range(6):
                        kk = tc * 6 + fc
                        if kk >= 2:
                            e.wait_ge(p.sem("ev"), ev0 + kk - 1)
                        p.inc(e.transpose(pT[kk % 2][:, :],
                                          mk_g[:, tc * 768 + fc * 128:tc * 768 + (fc + 1) * 128],
                                          g["ident_bf"][:, :]), "pe", 1)
        stB.close()
        p.barrier()

        for hl in range(4):
            h = gi * 4 + hl
            stH = ExitStack()
            pmb = [p.ps(f"l0_pmb{gi}{hl}{i}", (128, 128), F32, stH) for i in range(2)]
            pnd = [p.ps(f"l0_pnd{gi}{hl}{i}", (128, 129), F32, stH) for i in range(2)]
            se0 = p.n("se")
            ve0 = p.n("ve")
            pe0 = p.n("pe")
            wtk = wmbq.k
            mtk = memq.k
            with nc.Block() as blk:
                @blk.gpsimd
                def _(e):
                    wmbq.issue(e, wmb_h[h % 2][:, :].rearrange("p (dc o) -> p dc o", dc=NDC), g["d_wmb_hsw"][h].rearrange("dc p o -> p dc o"))
                    memq.issue(e, mem_sb[h % 2][:, :].rearrange("q (fc c) -> q fc c", fc=6), g["d_mem_aug"][h].rearrange("(fc q) c -> q fc c", q=128))

                @blk.tensor
                def _(e):
                    wmbq.wait(e, wtk + 1)
                    memq.wait(e, mtk + 1)
                    for tc in range(NTC):
                        if tc >= 2:
                            e.wait_ge(p.sem("se"), se0 + tc - 1)
                        for dc in range(NDC):
                            e.matmul(pmb[tc % 2][:, :],
                                     xnT[:, dc * T + tc * 128:dc * T + (tc + 1) * 128],
                                     wmb_h[h % 2][:, dc * 128:(dc + 1) * 128],
                                     start=(dc == 0), stop=False)
                        mm = e.matmul(pmb[tc % 2][:, :], g["ones_bf"][0:1, 0:128],
                                      bmb_sb[0:1, h * 128:(h + 1) * 128], start=False, stop=True)
                        p.inc(mm, "pe", 1)
                        if tc >= 2:
                            e.wait_ge(p.sem("ve"), ve0 + 4 * (tc - 1))
                        for fc in range(6):
                            mm = e.matmul(pnd[tc % 2][:, :],
                                          mkT_g[:, fc * T + tc * 128:fc * T + (tc + 1) * 128],
                                          mem_sb[h % 2][:, fc * 129:(fc + 1) * 129],
                                          start=(fc == 0), stop=(fc == 5))
                        p.inc(mm, "pe", 1)

                @blk.scalar
                def _(e):
                    for tc in range(NTC):
                        e.wait_ge(p.sem("pe"), pe0 + 2 * tc + 1)
                        p.inc(e.activation(mb_h[:, tc * 128:(tc + 1) * 128], pmb[tc % 2][:, :],
                                           AF.Sigmoid), "se", 1)

                @blk.vector
                def _(e):
                    for tc in range(NTC):
                        e.wait_ge(p.sem("pe"), pe0 + 2 * tc + 2)
                        e.wait_ge(p.sem("se"), se0 + tc + 1)
                        r = e.tensor_scalar_add(den_g[:, hl * 16 + tc:hl * 16 + tc + 1],
                                                pnd[tc % 2][:, 128:129], EPS)
                        p.edge(e, r, "vch")
                        r = e.reciprocal(rden_c[:, :], den_g[:, hl * 16 + tc:hl * 16 + tc + 1])
                        p.edge(e, r, "vch")
                        r = e.tensor_scalar_mul(scr1[:, :], pnd[tc % 2][:, 0:128], rden_c[:, 0:1])
                        p.edge(e, r, "vch")
                        r = e.tensor_sub(scr2[:, :],
                                         v_all[:, tc * 512 + gi * 128:tc * 512 + (gi + 1) * 128],
                                         scr1[:, :])
                        p.edge(e, r, "vch")
                        r = e.tensor_mul(wmv_g[:, tc * 512 + hl * 128:tc * 512 + (hl + 1) * 128],
                                         scr2[:, :], mb_h[:, tc * 128:(tc + 1) * 128])
                        p.inc(r, "ve", 4)
                    r = e.tensor_mul(scr1[:, 0:16], den_g[:, hl * 16:hl * 16 + 16], rmksq_g[:, :])
                    p.edge(e, r, "vch")
                    r = e.tensor_scalar(scr2[:, 0:16], scr1[:, 0:16], -1.0, 1.0, ALU.mult, ALU.add)
                    p.edge(e, r, "vch")
                    r = e.tensor_scalar(coef_v[:, :, hl], scr2[:, 0:16], 0.0, 1.0, ALU.max, ALU.min)
                    p.inc(r, "ve", 1)
            stH.close()
            p.barrier()

        stD = ExitStack()
        pdel = [p.ps(f"l0_pdel{gi}_{fc}", (128, 512), F32, stD) for fc in range(6)]
        pdn = [p.ps(f"l0_pdn{gi}_{i}", (128, 384), F32, stD) for i in range(2)]
        pe0 = p.n("pe")
        ve0 = p.n("ve")
        atk0 = arq.k
        with nc.Block() as blk:
            @blk.tensor
            def _(e):
                for tc in range(NTC):
                    for fc in range(6):
                        mm = e.matmul(pdel[fc][:, :],
                                      mk_g[:, tc * 768 + fc * 128:tc * 768 + (fc + 1) * 128],
                                      wmv_g[:, tc * 512:(tc + 1) * 512],
                                      start=(tc == 0), stop=(tc == NTC - 1))
                    for i2 in range(2):
                        mm = e.matmul(pdn[i2][0:4, :], coef_g[:, tc * 4:(tc + 1) * 4],
                                      mk_g[:, tc * 768 + i2 * 384:tc * 768 + (i2 + 1) * 384],
                                      start=(tc == 0), stop=(tc == NTC - 1))
                p.inc(mm, "pe", 1)

            @blk.vector
            def _(e):
                e.wait_ge(p.sem("pe"), pe0 + 1)
                for fc in range(6):
                    gidx = gi * 6 + fc
                    if gidx >= 2:
                        for t4 in range(4):
                            arq.wait(e, art0 + 4 * (gidx - 2) + t4 + 1)
                    p.inc(e.tensor_copy(dsb2[fc % 2][:, :], pdel[fc][:, :]), "ve", 1)
                if gi >= 1:
                    dnq.wait(e, dnt0 + gi)
                r = e.tensor_copy(dnsb[:, 0:384], pdn[0][0:4, :])
                p.edge(e, r, "vch")
                p.inc(e.tensor_copy(dnsb[:, 384:768], pdn[1][0:4, :]), "ve", 1)

            @blk.sync
            def _(e):
                for fc in range(6):
                    e.wait_ge(p.sem("ve"), ve0 + fc + 1)
                    for hl in range(4):
                        hh = gi * 4 + hl
                        arq.issue(e, ar_in[hh * 768 + fc * 128:hh * 768 + (fc + 1) * 128, :],
                                  dsb2[fc % 2][:, hl * 128:(hl + 1) * 128])
                e.wait_ge(p.sem("ve"), ve0 + 7)
                dnq.issue(e, ar_in[12288 + gi * 24:12288 + (gi + 1) * 24, :]
                          .rearrange("(hl r) c -> hl (r c)", r=6), dnsb[:, :])
        stD.close()
        p.barrier()
    with nc.Block() as blk:
        @blk.sync
        def _(e):
            arq.wait_all(e)
            dnq.wait_all(e)
    p.barrier()
    st.close()



def emit_ar(p, ar_in, ar_out):
    arq = p.dq("s4", 8)
    dnq = p.dq("s2", 4)
    with p.nc.Block() as blk:
        @blk.gpsimd
        def _(e):
            arq.wait_all(e)
            dnq.wait_all(e)
            cc = e.collective_compute(
                "AllReduce", ALU.add, replica_groups=[list(range(NCORES))],
                ins=[ar_in.opt()], outs=[ar_out.opt()])
            p.inc(cc, "cc", 1)
    p.barrier()


def emit_newmem(p, g, ar_out, nm_aug_sb):
    nc = p.nc
    st = ExitStack()
    dn_in = p.sb("nm_dnin", (128, 128), F32, st)
    dnT = p.sb("nm_dnT", (128, 128), F32, st)
    pdT = p.ps("nm_pdT", (128, 128), F32, st)
    mem_bf = p.sb("nm_mem", (128, 96 * 129), BF16, st)
    dar = p.sb("nm_dar", (128, 96 * 128), F32, st)
    q = p.dq("s4", 8)
    qg = p.dq("g2", 4)
    qt0 = q.k
    qg0 = qg.k
    pe0 = p.n("pe")
    ve0 = p.n("ve")
    vz0 = p.n("vz")
    with nc.Block() as blk:
        @blk.gpsimd
        def _(e):
            for i in range(4):
                qg.issue(e, mem_bf[:, i * 24 * 129:(i + 1) * 24 * 129]
                         .rearrange("q (j c) -> q j c", j=24),
                         g["d_mem_aug"].rearrange("h (fc q) c -> q (h fc) c", q=128)
                         [:, i * 24:(i + 1) * 24, :])

        @blk.sync
        def _(e):
            e.wait_ge(p.sem("cc"), p.n("cc"))
            e.wait_ge(p.sem("vz"), vz0 + 1)
            q.issue(e, dn_in[0:96, :], ar_out[12288:12384, :])
            for i in range(4):
                q.issue(e, dar[:, i * 24 * 128:(i + 1) * 24 * 128]
                        .rearrange("q (j d) -> q j d", j=24),
                        ar_out[0:12288, :].rearrange("(j q) d -> q j d", q=128)
                        [:, i * 24:(i + 1) * 24, :])

        @blk.tensor
        def _(e):
            q.wait(e, qt0 + 1)
            p.inc(e.transpose(pdT[:, :], dn_in[:, :], g["ident_f32"][:, :]), "pe", 1)

        @blk.vector
        def _(e):
            p.inc(e.memset(dn_in[:, :], 0.0), "vz", 1)
            e.wait_ge(p.sem("pe"), pe0 + 1)
            r = e.tensor_copy(dnT[:, :], pdT[:, :])
            p.edge(e, r, "vch")
            for i in range(4):
                qg.wait(e, qg0 + i + 1)
                q.wait(e, qt0 + i + 2)
            for j in range(96):
                r = e.tensor_add(nm_aug_sb[:, j * 129:j * 129 + 128],
                                 mem_bf[:, j * 129:j * 129 + 128], dar[:, j * 128:(j + 1) * 128])
                p.edge(e, r, "vch")
                r = e.tensor_add(nm_aug_sb[:, j * 129 + 128:j * 129 + 129],
                                 mem_bf[:, j * 129 + 128:j * 129 + 129], dnT[:, j:j + 1])
                p.edge(e, r, "vch")
            p.inc(e.drain(), "ve", 1)
    st.close()
    p.barrier()



def emit_l0p2(p, xnT, g, nm_aug, x2d, g0_dram, x0_dram, x1_dram):
    nc = p.nc
    st = ExitStack()
    wq_sb = [p.sb(f"p2_wq{i}", (128, NDC * 128), BF16, st) for i in range(2)]
    wg_sb = p.sb("p2_wg", (128, NDC * 512), BF16, st)
    gate_hb = p.sb("p2_gate", (128, NTC * 512), BF16, st)
    g0_hb = p.sb("p2_g0", (128, NTC * 512), BF16, st)
    x0t = [p.sb(f"p2_x0t{i}", (128, 128), F32, st) for i in range(4)]
    bq_cols = p.sb("p2_bqc", (128, 16), F32, st)
    nbq_cols = p.sb("p2_nbqc", (128, 16), F32, st)
    bg_row = p.sb("p2_bgr", (1, 2048), BF16, st)
    x2a = p.sb("p2_x2a", (128, T), BF16, st)
    x2b = p.sb("p2_x2b", (128, T), BF16, st)
    x2r = [p.sb("p2_x2r0", (128, T), BF16, st)] * 2
    dT = p.sb("p2_dT", (128, 6 * T), BF16, st)
    dsq = [p.sb("p2_dsq0", (128, T), BF16, st)] * 2
    rnrow = p.sb("p2_rnrow", (1, T), F32, st)
    rn16 = p.sb("p2_rn16", (16, 128), F32, st)
    rnT = p.sb("p2_rnT", (128, 16), F32, st)
    dfin = p.sb("p2_dfin", (128, 1), F32, st)
    rden = p.sb("p2_rden", (128, 1), F32, st)
    rr = p.sb("p2_rr", (128, 1), F32, st)
    memh = p.sb("p2_memh", (128, 128), F32, st)
    sc1 = p.sb("p2_sc1", (128, 128), F32, st)
    sc2 = [p.sb(f"p2_sc2{i}", (128, 128), F32, st) for i in range(2)]
    sc3 = p.sb("p2_sc3", (128, 128), F32, st)
    x1t = [p.sb(f"p2_x1t{i}", (128, 128), F32, st) for i in range(2)]

    g["d_rnscr"] = p.dram("d_rnscr", (1, T), F32).ap()
    wqq = p.dq("g1", 8)
    wgq = p.dq("g1", 8)
    hbq = p.dq("s5", 8)
    xdq = p.dq("s1", 4)
    xrq = p.dq("s1", 4)
    x1q = p.dq("s3", 4)
    x0q = p.dq("s4", 8)
    qvq = p.dq("s2", 4)

    cst = p.dq("s2", 4)
    cbg = p.dq("g2", 4)
    ct0 = cst.k
    with nc.Block() as blk:
        @blk.sync
        def _(e):
            cst.issue(e, bq_cols[:, :], g["d_bq_cols"][:, :])

        @blk.gpsimd
        def _(e):
            cbg.issue(e, bg_row[:, :], g["d_bg_row"][:, :])
            cbg.wait_all(e)

        @blk.vector
        def _(e):
            cst.wait(e, ct0 + 1)
            p.inc(e.tensor_scalar_mul(nbq_cols[:, :], bq_cols[:, :], -1.0), "ve", 1)
    p.barrier()

    stP = ExitStack()
    pq = [p.ps(f"p2_pq{i}", (128, 512), F32, stP) for i in range(4)]
    pnd = [p.ps(f"p2_pnd{i}", (128, 129), F32, stP) for i in range(2)]
    pxT = [p.ps(f"p2_pxT{i}", (128, 128), BF16, stP) for i in range(2)]
    xn_tok = [p.sb(f"p2_xnt{i}", (128, 128), BF16, st) for i in range(2)]

    PEH = 61
    SEH = 13
    VEH = 17
    VPH = 16
    GEH = 16
    VQH = 6

    for hb in range(4):
        se0 = p.n("se")
        pe0 = p.n("pe")
        wgt0 = wgq.k
        with nc.Block() as blk:
            @blk.gpsimd
            def _(e):
                wgq.issue(e, wg_sb[:, :].rearrange("p (dc o) -> p dc o", dc=NDC),
                          g["d_wg_gsw"][hb].rearrange("dc p o -> p dc o"))

            @blk.sync
            def _(e):
                for tc in range(NTC):
                    hbq.issue(e, g0_hb[:, tc * 512:(tc + 1) * 512],
                              g0_dram[tc * 128:(tc + 1) * 128, hb * 512:(hb + 1) * 512])

            @blk.tensor
            def _(e):
                wgq.wait(e, wgt0 + 1)
                for tc in range(NTC):
                    if tc >= 2:
                        e.wait_ge(p.sem("se"), se0 + tc - 1)
                    for dc in range(NDC):
                        e.matmul(pq[tc % 2][:, :],
                                 xnT[:, dc * T + tc * 128:dc * T + (tc + 1) * 128],
                                 wg_sb[:, dc * 512:(dc + 1) * 512], start=(dc == 0), stop=False)
                    mm = e.matmul(pq[tc % 2][:, :], g["ones_bf"][0:1, 0:128],
                                  bg_row[0:1, hb * 512:(hb + 1) * 512], start=False, stop=True)
                    p.inc(mm, "pe", 1)

            @blk.scalar
            def _(e):
                for tc in range(NTC):
                    e.wait_ge(p.sem("pe"), pe0 + tc + 1)
                    p.inc(e.activation(gate_hb[:, tc * 512:(tc + 1) * 512], pq[tc % 2][:, :],
                                       AF.Sigmoid), "se", 1)
        p.barrier()

        seB = p.n("se")
        peB = p.n("pe")
        veB = p.n("ve")
        vpB = p.n("vp")
        geB = p.n("ge")
        vqB = p.n("vq")
        wqt0 = wqq.k
        xdt0 = xdq.k
        xrt0 = xrq.k
        qvt0 = qvq.k
        x1t0 = x1q.k
        x0t0 = x0q.k
        with nc.Block() as blk:
            @blk.gpsimd
            def _(e):
                for hl in range(2):
                    wqq.issue(e, wq_sb[hl % 2][:, :].rearrange("p (dc o) -> p dc o", dc=NDC),
                              g["d_wq_hsw"][hb * 4 + hl].rearrange("dc p o -> p dc o"))
                hbq.wait_all(e)
                for hl in range(4):
                    h = hb * 4 + hl
                    if hl + 2 < 4:
                        e.wait_ge(p.sem("pe"), peB + hl * PEH + 4)
                        wqq.issue(e, wq_sb[hl % 2][:, :].rearrange("p (dc o) -> p dc o", dc=NDC),
                                  g["d_wq_hsw"][hb * 4 + hl + 2].rearrange("dc p o -> p dc o"))
                    for tc in range(NTC):
                        gtc = hl * 16 + tc
                        e.wait_ge(p.sem("ve"), veB + hl * VEH + 1 + tc + 1)
                        if gtc >= 1:
                            e.wait_ge(p.sem("ge"), geB + gtc)
                        r = e.tensor_mul(sc3[:, :], sc2[tc % 2][:, :],
                                         g0_hb[:, tc * 512 + hl * 128:tc * 512 + (hl + 1) * 128])
                        p.edge(e, r, "gch")
                        xt1 = x1t0 + gtc - 1
                        if xt1 >= 1:
                            x1q.wait(e, xt1)
                        x0q.wait(e, x0t0 + gtc + 1)
                        r = e.tensor_add(x1t[tc % 2][:, :], sc3[:, :], x0t[tc % 4][:, :])
                        p.inc(r, "ge", 1)

            @blk.tensor
            def _(e):
                for hl in range(4):
                    h = hb * 4 + hl
                    wqq.wait(e, wqt0 + hl + 1)
                    for sp in range(4):
                        if hl >= 1:
                            e.wait_ge(p.sem("se"), seB + (hl - 1) * SEH + 8 + sp + 1)
                        for dc in range(NDC):
                            mm = e.matmul(pq[sp][:, :], wq_sb[hl % 2][:, dc * 128:(dc + 1) * 128],
                                          xnT[:, dc * T + sp * 512:dc * T + (sp + 1) * 512],
                                          start=(dc == 0), stop=(dc == NDC - 1))
                        p.inc(mm, "pe", 1)
                    for fc in range(6):
                        e.wait_ge(p.sem("vq"), vqB + hl * VQH + fc + 1)
                        for sp in range(4):
                            mm = e.matmul(pq[sp][0:1, :], g["ones_col"][:, 0:1],
                                          dsq[fc % 2][:, sp * 512:(sp + 1) * 512],
                                          start=(fc == 0), stop=(fc == 5))
                            p.inc(mm, "pe", 1)
                    qvq.wait(e, qvt0 + 2 * hl + 2)
                    p.inc(e.transpose(pnd[0][:, 0:16], rn16[:, :], g["ident_f32"][0:16, 0:16]), "pe", 1)
                    e.wait_ge(p.sem("se"), seB + hl * SEH + 13)
                    for tc in range(NTC):
                        gtc = hl * 16 + tc
                        if gtc >= 2:
                            e.wait_ge(p.sem("vp"), vpB + gtc - 1)
                        for fc in range(6):
                            mm = e.matmul(pnd[tc % 2][:, :],
                                          dT[:, fc * T + tc * 128:fc * T + (tc + 1) * 128],
                                          nm_aug[:, (h * 6 + fc) * 129:(h * 6 + fc + 1) * 129],
                                          start=(fc == 0), stop=(fc == 5))
                        p.inc(mm, "pe", 1)
                        p.inc(e.transpose(pxT[tc % 2][:, :],
                                          xnT[:, h * T + tc * 128:h * T + (tc + 1) * 128],
                                          g["ident_bf"][:, :]), "pe", 1)

            @blk.scalar
            def _(e):
                for hl in range(4):
                    h = hb * 4 + hl
                    for sp in range(4):
                        e.wait_ge(p.sem("pe"), peB + hl * PEH + sp + 1)
                        e.wait_ge(p.sem("vq"), vqB + hl * VQH)
                        p.inc(e.activation(x2a[:, sp * 512:(sp + 1) * 512], pq[sp][:, :],
                                           AF.Relu, bias=bq_cols[:, h:h + 1]), "se", 1)
                        p.inc(e.activation(x2b[:, sp * 512:(sp + 1) * 512], pq[sp][:, :],
                                           AF.Relu, bias=nbq_cols[:, h:h + 1],
                                           scale=g["neg1_col"][:, 0:1]), "se", 1)
                    if hl >= 1:
                        qvq.wait(e, qvt0 + 2 * (hl - 1) + 1)
                        e.wait_ge(p.sem("ve"), veB + (hl - 1) * VEH + 1)
                    for sp in range(4):
                        e.wait_ge(p.sem("pe"), peB + hl * PEH + 4 + 20 + sp + 1)
                        p.inc(e.activation(rnrow[0:1, sp * 512:(sp + 1) * 512], pq[sp][0:1, :],
                                           AF.Sqrt), "se", 1)
                    e.wait_ge(p.sem("pe"), peB + hl * PEH + 4 + 24 + 1)
                    if hl >= 1:
                        e.wait_ge(p.sem("ve"), veB + hl * VEH)
                    p.inc(e.activation(rnT[:, :], pnd[0][:, 0:16], AF.Copy), "se", 1)

            @blk.sync
            def _(e):
                for hl in range(4):
                    h = hb * 4 + hl
                    e.wait_ge(p.sem("se"), seB + hl * SEH + 8)
                    e.wait_ge(p.sem("vq"), vqB + hl * VQH)
                    xdq.issue(e, x2d[0:128, :], x2a[:, :])
                    xdq.issue(e, x2d[128:256, :], x2b[:, :])
                    xdq.issue(e, x2d[256:384, :], x2a[:, :])
                    xdq.issue(e, x2d[384:512, :], x2b[:, :])
                    for t4 in range(4):
                        xdq.wait(e, xdt0 + 10 * hl + t4 + 1)
                    for fc in range(6):
                        j = fc // 2 + 1
                        sr = 256 + (fc % 2) * 128 - j
                        if fc >= 1:
                            e.wait_ge(p.sem("vq"), vqB + hl * VQH + fc)
                        xrq.issue(e, x2r[fc % 2][:, :], x2d[sr:sr + 128, :])
                    e.wait_ge(p.sem("ve"), veB + hl * VEH + 1)
                    t1_ = qvq.issue(e, g["d_rnscr"][:, :], rnrow[0:1, :])
                    qvq.wait(e, t1_)
                    if hl >= 1:
                        e.wait_ge(p.sem("pe"), peB + (hl - 1) * PEH + 29)
                    qvq.issue(e, rn16[:, :], g["d_rnscr"].rearrange("q (tc c) -> (q tc) c", tc=16))
                    for tc in range(4):
                        gtc = hl * 16 + tc
                        x0q.issue(e, x0t[tc][:, :],
                                  x0_dram[tc * 128:(tc + 1) * 128, h * 128:(h + 1) * 128])
                    for tc in range(NTC):
                        gtc = hl * 16 + tc
                        e.wait_ge(p.sem("ge"), geB + gtc + 1)
                        x1q.issue(e, x1_dram[tc * 128:(tc + 1) * 128, h * 128:(h + 1) * 128],
                                  x1t[tc % 2][:, :])
                        if tc + 4 < NTC:
                            x0q.issue(e, x0t[(tc + 4) % 4][:, :],
                                      x0_dram[(tc + 4) * 128:(tc + 5) * 128, h * 128:(h + 1) * 128])

            @blk.vector
            def _(e):
                for hl in range(4):
                    h = hb * 4 + hl
                    xr0 = xrt0 + 10 * hl + 4
                    for fc in range(6):
                        xrq.wait(e, xr0 + fc + 1)
                        src = x2a if fc % 2 == 0 else x2b
                        if fc == 0 and hl >= 1:
                            e.wait_ge(p.sem("pe"), peB + hl * PEH)
                        r = e.tensor_mul(dT[:, fc * T:(fc + 1) * T], src[:, :],
                                         x2r[fc % 2][:, :])
                        p.edge(e, r, "vch")
                        if fc >= 1:
                            e.wait_ge(p.sem("pe"), peB + hl * PEH + 4 + 4 * fc)
                        elif hl >= 1:
                            e.wait_ge(p.sem("pe"), peB + (hl - 1) * PEH + 28)
                        p.inc(e.tensor_mul(dsq[fc % 2][:, :], dT[:, fc * T:(fc + 1) * T],
                                           dT[:, fc * T:(fc + 1) * T]), "vq", 1)
                    e.wait_ge(p.sem("se"), seB + hl * SEH + 12)
                    r = e.tensor_scalar_max(rnrow[:, :], rnrow[:, :], 1e-12)
                    p.edge(e, r, "vch")
                    p.inc(e.reciprocal(rnrow[:, :], rnrow[:, :]), "ve", 1)
                    for tc in range(NTC):
                        gtc = hl * 16 + tc
                        e.wait_ge(p.sem("pe"), peB + hl * PEH + 29 + 2 * tc + 2)
                        if gtc >= 2:
                            e.wait_ge(p.sem("ge"), geB + gtc - 1)
                        r = e.tensor_copy(xn_tok[tc % 2][:, :], pxT[tc % 2][:, :])
                        p.edge(e, r, "vch")
                        r = e.tensor_scalar(dfin[:, :], pnd[tc % 2][:, 128:129],
                                            rnT[:, tc:tc + 1], EPS, ALU.mult, ALU.add)
                        p.edge(e, r, "vch")
                        r = e.reciprocal(rden[:, :], dfin[:, :])
                        p.edge(e, r, "vch")
                        r = e.tensor_mul(rr[:, :], rnT[:, tc:tc + 1], rden[:, :])
                        p.edge(e, r, "vch")
                        r = e.tensor_scalar_mul(memh[:, :], pnd[tc % 2][:, 0:128], rr[:, 0:1])
                        p.edge(e, r, "vp")
                        r = e.tensor_mul(sc1[:, :], gate_hb[:, tc * 512 + hl * 128:tc * 512 + (hl + 1) * 128],
                                         memh[:, :])
                        p.edge(e, r, "vch")
                        r = e.tensor_add(sc2[tc % 2][:, :], sc1[:, :], xn_tok[tc % 2][:, :])
                        p.inc(r, "ve", 1)
        p.barrier()
    stP.close()
    with nc.Block() as blk:
        @blk.sync
        def _(e):
            x1q.wait_all(e)
    p.barrier()
    st.close()



def emit_l1kv(p, xnT, g, ag_in, ag_out):
    nc = p.nc
    st = ExitStack()
    wk1 = [p.sb(f"kv_wk{i}", (128, NDC * 128), BF16, st) for i in range(2)]
    wv1 = p.sb("kv_wv", (128, NDC * 2048), BF16, st)
    bk1r = p.sb("kv_bk1r", (1, 2048), BF16, st)
    bv1 = p.sb("kv_bv1", (1, 2048), BF16, st)
    kt_sb = [p.sb(f"kv_kt{i}", (128, T), BF16, st) for i in range(2)]
    vt_sb = [p.sb(f"kv_vt{i}", (128, T), BF16, st) for i in range(2)]
    wq1 = p.dq("g1", 8)
    agq = p.dq("s5", 8)

    stK = ExitStack()
    pk = [p.ps(f"kv_pk{i}", (128, 512), F32, stK) for i in range(4)]
    se0 = p.n("se")
    pe0 = p.n("pe")
    wt0 = wq1.k
    akt0 = agq.k
    with nc.Block() as blk:
        @blk.gpsimd
        def _(e):
            kvb = p.dq("g2", 4)
            kvb.issue(e, bv1[:, :], g["d_bv1_row"][:, :])
            kvb.issue(e, bk1r[:, :], g["d_bk1_rows"][:, :])
            for oc in range(16):
                if oc >= 2:
                    e.wait_ge(p.sem("pe"), pe0 + 4 * (oc - 1))
                wq1.issue(e, wk1[oc % 2][:, :].rearrange("p (dc o) -> p dc o", dc=NDC),
                          g["d_wk1_hsw"][oc].rearrange("dc p o -> p dc o"))

        @blk.tensor
        def _(e):
            p.dq("g2", 4).wait_all(e)
            for oc in range(16):
                wq1.wait(e, wt0 + oc + 1)
                for sp in range(4):
                    if oc >= 1:
                        e.wait_ge(p.sem("se"), se0 + 4 * (oc - 1) + sp + 1)
                    for dc in range(NDC):
                        e.matmul(pk[sp][:, :], wk1[oc % 2][:, dc * 128:(dc + 1) * 128],
                                 xnT[:, dc * T + sp * 512:dc * T + (sp + 1) * 512],
                                 start=(dc == 0), stop=False)
                    mm = e.matmul(pk[sp][:, :], bk1r[0:1, oc * 128:(oc + 1) * 128], g["ones_bf"][0:1, :],
                                  start=False, stop=True)
                    p.inc(mm, "pe", 1)

        @blk.scalar
        def _(e):
            for oc in range(16):
                for sp in range(4):
                    e.wait_ge(p.sem("pe"), pe0 + 4 * oc + sp + 1)
                    if oc >= 2 and sp == 0:
                        agq.wait(e, akt0 + oc - 1)
                    p.inc(e.activation(kt_sb[oc % 2][:, sp * 512:(sp + 1) * 512], pk[sp][:, :],
                                       AF.Copy), "se", 1)

        @blk.sync
        def _(e):
            for oc in range(16):
                e.wait_ge(p.sem("se"), se0 + 4 * (oc + 1))
                agq.issue(e, ag_in[oc * 128:(oc + 1) * 128, :], kt_sb[oc % 2][:, :])
    stK.close()
    p.barrier()

    stV = ExitStack()
    pv = [p.ps(f"kv_pv{i}_{j}", (128, 512), F32, stV) for i in range(2) for j in range(4)]
    se0 = p.n("se")
    pe0 = p.n("pe")
    wt0 = wq1.k
    agt0 = agq.k
    with nc.Block() as blk:
        @blk.gpsimd
        def _(e):
            wq1.issue(e, wv1[:, :].rearrange("p (dc o) -> p dc o", dc=NDC),
                      g["d_wv1_sw"].rearrange("dc p o -> p dc o"))

        @blk.tensor
        def _(e):
            wq1.wait(e, wt0 + 1)
            for tc in range(NTC):
                if tc >= 2:
                    e.wait_ge(p.sem("se"), se0 + 4 * (tc - 1))
                for ob in range(4):
                    pvx = pv[(tc % 2) * 4 + ob]
                    for dc in range(NDC):
                        e.matmul(pvx[:, :], xnT[:, dc * T + tc * 128:dc * T + (tc + 1) * 128],
                                 wv1[:, dc * 2048 + ob * 512:dc * 2048 + (ob + 1) * 512],
                                 start=(dc == 0), stop=False)
                    mm = e.matmul(pvx[:, :], g["ones_bf"][0:1, 0:128],
                                  bv1[0:1, ob * 512:(ob + 1) * 512], start=False, stop=True)
                    p.inc(mm, "pe", 1)

        @blk.scalar
        def _(e):
            for tc in range(NTC):
                if tc >= 2:
                    agq.wait(e, agt0 + tc - 1)
                for ob in range(4):
                    e.wait_ge(p.sem("pe"), pe0 + 4 * tc + ob + 1)
                    p.inc(e.activation(vt_sb[tc % 2][:, ob * 512:(ob + 1) * 512],
                                       pv[(tc % 2) * 4 + ob][:, :], AF.Copy), "se", 1)

        @blk.sync
        def _(e):
            for tc in range(NTC):
                e.wait_ge(p.sem("se"), se0 + 4 * (tc + 1))
                agq.issue(e, ag_in[2048 + tc * 128:2048 + (tc + 1) * 128, :], vt_sb[tc % 2][:, :])
    stV.close()
    p.barrier()
    st.close()

    with nc.Block() as blk:
        @blk.gpsimd
        def _(e):
            agq.wait_all(e)
            cc = e.collective_compute(
                "AllGather", ALU.bypass,
                replica_groups=[[0, 1], [2, 3], [4, 5], [6, 7]],
                ins=[ag_in.opt()], outs=[ag_out.opt()])
            p.inc(cc, "cc", 1)
            e.wait_ge(p.sem("cc"), p.n("cc"))
    p.barrier()



def emit_ffn1(p, xnT, g, g1_dram):
    nc = p.nc
    NI = 44
    st = ExitStack()
    fuse = p.sb("f1_fuse", (128, NI * 1024), BF16, st)
    wgs = [p.sb(f"f1_wg{i}", (128, 2 * DM), BF16, st) for i in range(2)]
    wus = [p.sb(f"f1_wu{i}", (128, 2 * DM), BF16, st) for i in range(2)]
    wds = [p.sb(f"f1_wd{i}", (128, 1024), BF16, st) for i in range(2)]
    sg = p.sb("f1_sg", (128, NI), F32, st)
    su = p.sb("f1_su", (128, NI), F32, st)
    sdb = p.sb("f1_sdb", (128, 512), F32, st)
    gsc = [p.sb(f"f1_gsc{i}", (128, 1024), BF16, st) for i in range(2)]
    usc = [p.sb(f"f1_usc{i}", (128, 1024), BF16, st) for i in range(2)]
    ysc = [p.sb("f1_ysc0", (128, 512), F32, st)] * 2
    g1t = [p.sb("f1_g1t0", (128, 512), BF16, st)] * 2

    scq = p.dq("s2", 4)
    with nc.Block() as blk:
        @blk.sync
        def _(e):
            scq.issue(e, sg[:, :], g["d_sg1_sw"][:, :])
            scq.issue(e, su[:, :], g["d_su1_sw"][:, :])
            scq.wait_all(e)
    p.barrier()

    wq = p.dq("g1", 8)
    wdq = p.dq("g1", 8)
    g1q = p.dq("s3", 4)
    for half in range(2):
        t0 = half * 1024
        stq = ExitStack()
        pg = [[p.ps(f"f1_pg{half}_{i}_{s2}", (128, 512), F32, stq) for s2 in range(2)] for i in range(2)]
        pu = [[p.ps(f"f1_pu{half}_{i}_{s2}", (128, 512), F32, stq) for s2 in range(2)] for i in range(2)]
        se0 = p.n("se")
        ve0 = p.n("ve")
        vw0 = p.n("vw")
        pe0 = p.n("pe")
        wt0 = wq.k
        with nc.Block() as blk:
            @blk.gpsimd
            def _(e):
                for j in range(NI // 2):
                    if j >= 2:
                        e.wait_ge(p.sem("pe"), pe0 + 8 * (j - 1))
                    wq.issue(e, wgs[j % 2][:, :].rearrange("p (i o) -> p i o", i=2),
                             g["d_wg1_sw"][2 * j:2 * j + 2].rearrange("i p o -> p i o"))
                    wq.issue(e, wus[j % 2][:, :].rearrange("p (i o) -> p i o", i=2),
                             g["d_wu1_sw"][2 * j:2 * j + 2].rearrange("i p o -> p i o"))

            @blk.vector
            def _(e):
                for j in range(NI // 2):
                    wq.wait(e, wt0 + 2 * j + 1)
                    wq.wait(e, wt0 + 2 * j + 2)
                    if j >= 1:
                        e.wait_ge(p.sem("pe"), pe0 + 8 * j)
                    r = e.tensor_scalar_add(wgs[j % 2][:, :], wgs[j % 2][:, :], -128.0)
                    p.inc(r, "vw", 1)
                    r = e.tensor_scalar_add(wus[j % 2][:, :], wus[j % 2][:, :], -128.0)
                    p.inc(r, "vw", 1)
                    for ic in (2 * j - 2, 2 * j - 1):
                        if ic >= 0:
                            e.wait_ge(p.sem("se"), se0 + 4 * (ic + 1))
                            p.inc(e.tensor_mul(fuse[:, ic * 1024:(ic + 1) * 1024],
                                               gsc[ic % 2][:, :], usc[ic % 2][:, :]), "ve", 1)
                for ic in (NI - 2, NI - 1):
                    e.wait_ge(p.sem("se"), se0 + 4 * (ic + 1))
                    p.inc(e.tensor_mul(fuse[:, ic * 1024:(ic + 1) * 1024],
                                       gsc[ic % 2][:, :], usc[ic % 2][:, :]), "ve", 1)

            @blk.tensor
            def _(e):
                for ic in range(NI):
                    j, jo = ic // 2, (ic % 2) * DM
                    e.wait_ge(p.sem("vw"), vw0 + 2 * (j + 1))
                    if ic >= 2:
                        e.wait_ge(p.sem("se"), se0 + 4 * (ic - 1))
                    for s2 in range(2):
                        for dc in range(NDC):
                            mm = e.matmul(pg[ic % 2][s2][:, :], wgs[j % 2][:, jo + dc * 128:jo + (dc + 1) * 128],
                                          xnT[:, dc * T + t0 + s2 * 512:dc * T + t0 + (s2 + 1) * 512],
                                          start=(dc == 0), stop=(dc == NDC - 1))
                        p.inc(mm, "pe", 1)
                    for s2 in range(2):
                        for dc in range(NDC):
                            mm = e.matmul(pu[ic % 2][s2][:, :], wus[j % 2][:, jo + dc * 128:jo + (dc + 1) * 128],
                                          xnT[:, dc * T + t0 + s2 * 512:dc * T + t0 + (s2 + 1) * 512],
                                          start=(dc == 0), stop=(dc == NDC - 1))
                        p.inc(mm, "pe", 1)

            @blk.scalar
            def _(e):
                for ic in range(NI):
                    if ic >= 2:
                        e.wait_ge(p.sem("ve"), ve0 + ic)
                    for s2 in range(2):
                        e.wait_ge(p.sem("pe"), pe0 + 4 * ic + s2 + 1)
                        p.inc(e.activation(gsc[ic % 2][:, s2 * 512:(s2 + 1) * 512],
                                           pg[ic % 2][s2][:, :], AF.Silu, scale=sg[:, ic:ic + 1]), "se", 1)
                    for s2 in range(2):
                        e.wait_ge(p.sem("pe"), pe0 + 4 * ic + s2 + 3)
                        p.inc(e.activation(usc[ic % 2][:, s2 * 512:(s2 + 1) * 512],
                                           pu[ic % 2][s2][:, :], AF.Copy, scale=su[:, ic:ic + 1]), "se", 1)
        stq.close()
        p.barrier()

        for oq in range(4):
            std = ExitStack()
            pd = [p.ps(f"f1_pd{half}_{oq}_{t8}", (128, 512), F32, std) for t8 in range(8)]
            pe0 = p.n("pe")
            ve0 = p.n("ve")
            vw0 = p.n("vw")
            se0 = p.n("se")
            dt0 = wdq.k
            gt0 = g1q.k
            sd_tk = scq.k + 1
            with nc.Block() as blk:
                @blk.gpsimd
                def _(e):
                    for j in range(NI // 2):
                        if j >= 2:
                            e.wait_ge(p.sem("pe"), pe0 + 2 * (j - 1))
                        wdq.issue(e, wds[j % 2][:, :].rearrange("p (i o) -> p i o", i=2),
                                 g["d_wd1_sw"][oq, 2 * j:2 * j + 2].rearrange("i p o -> p i o"))

                @blk.vector
                def _(e):
                    for j in range(NI // 2):
                        wdq.wait(e, dt0 + j + 1)
                        if j >= 1:
                            e.wait_ge(p.sem("pe"), pe0 + 2 * j)
                        p.inc(e.tensor_scalar_add(wds[j % 2][:, :], wds[j % 2][:, :], -128.0), "vw", 1)
                    scq.wait(e, sd_tk)
                    e.wait_ge(p.sem("pe"), pe0 + NI)
                    for t8 in range(8):
                        if t8 >= 1:
                            e.wait_ge(p.sem("se"), se0 + t8)
                        p.inc(e.tensor_mul(ysc[t8 % 2][:, :], pd[t8][:, :],
                                           sdb[:, 0:512]), "ve", 1)

                @blk.tensor
                def _(e):
                    for ic in range(NI):
                        j, jo = ic // 2, (ic % 2) * 512
                        e.wait_ge(p.sem("vw"), vw0 + j + 1)
                        for t8 in range(8):
                            mm = e.matmul(pd[t8][:, :],
                                          fuse[:, ic * 1024 + t8 * 128:ic * 1024 + (t8 + 1) * 128],
                                          wds[j % 2][:, jo:jo + 512],
                                          start=(ic == 0), stop=(ic == NI - 1))
                        p.inc(mm, "pe", 1)

                @blk.scalar
                def _(e):
                    for t8 in range(8):
                        e.wait_ge(p.sem("ve"), ve0 + t8 + 1)
                        gl = gt0 + t8
                        if gl >= 1:
                            g1q.wait(e, gl)
                        p.inc(e.activation(g1t[t8 % 2][:, :], ysc[t8 % 2][:, :], AF.Silu), "se", 1)

                @blk.sync
                def _(e):
                    scq.issue(e, sdb[:, :], g["d_sd1_bc"][:, 0:1, :] if False else g["d_sd1_bc"][:, oq * 512:(oq + 1) * 512])
                    for t8 in range(8):
                        e.wait_ge(p.sem("se"), se0 + t8 + 1)
                        g1q.issue(e, g1_dram[t0 + t8 * 128:t0 + (t8 + 1) * 128,
                                             oq * 512:(oq + 1) * 512], g1t[t8 % 2][:, :])
            std.close()
            p.barrier()
    with nc.Block() as blk:
        @blk.sync
        def _(e):
            g1q.wait_all(e)
    p.barrier()
    st.close()



def emit_l1q(p, xnT, g, attnout):
    nc = p.nc
    st = ExitStack()
    wqh = [p.sb(f"at_wq{i}", (128, NDC * 128), BF16, st) for i in range(2)]
    bq1r = p.sb("at_bq1r", (1, 2048), BF16, st)
    wq2 = p.dq("g1", 8)
    cq = p.dq("g2", 4)
    stQ = ExitStack()
    pq = [p.ps(f"at_pq{i}", (128, 512), F32, stQ) for i in range(4)]
    se0 = p.n("se")
    pe0 = p.n("pe")
    wt0 = wq2.k
    ct0 = cq.k
    with nc.Block() as blk:
        @blk.gpsimd
        def _(e):
            cq.issue(e, bq1r[:, :], g["d_bq1_rows"][:, :])
            for h in range(16):
                if h >= 2:
                    e.wait_ge(p.sem("pe"), pe0 + 4 * (h - 1))
                wq2.issue(e, wqh[h % 2][:, :].rearrange("p (dc o) -> p dc o", dc=NDC),
                          g["d_wq1_hsw"][h].rearrange("dc p o -> p dc o"))

        @blk.tensor
        def _(e):
            cq.wait(e, ct0 + 1)
            for h in range(16):
                wq2.wait(e, wt0 + h + 1)
                for sp in range(4):
                    k4 = 4 * h + sp
                    if k4 >= 4:
                        e.wait_ge(p.sem("se"), se0 + k4 - 3)
                    for dc in range(NDC):
                        e.matmul(pq[sp][:, :], wqh[h % 2][:, dc * 128:(dc + 1) * 128],
                                 xnT[:, dc * T + sp * 512:dc * T + (sp + 1) * 512],
                                 start=(dc == 0), stop=False)
                    mm = e.matmul(pq[sp][:, :], bq1r[0:1, h * 128:(h + 1) * 128],
                                  g["ones_bf"][0:1, :], start=False, stop=True)
                    p.inc(mm, "pe", 1)

        @blk.scalar
        def _(e):
            for h in range(16):
                for sp in range(4):
                    k4 = 4 * h + sp
                    e.wait_ge(p.sem("pe"), pe0 + k4 + 1)
                    p.inc(e.activation(attnout[:, h * 2048 + sp * 512:h * 2048 + (sp + 1) * 512],
                                       pq[sp][:, :], AF.Copy), "se", 1)
    stQ.close()
    p.barrier()
    st.close()


def emit_l1attn(p, g, ag_out, attnout):
    nc = p.nc
    SQ = 4096
    st = ExitStack()
    kh = [p.sb(f"at_k{i}", (128, SQ), BF16, st) for i in range(2)]
    vh = [p.sb(f"at_v{i}", (128, SQ), BF16, st) for i in range(2)]
    probs = [p.sb("at_probs0", (128, 32 * 512), BF16, st)] * 2
    rdrow = p.sb("at_rdrow", (1, 512), F32, st)
    rdb = p.sb("at_rdb", (128, 512), F32, st)
    psc = [p.ps(f"at_psc{i}", (128, 1024), F32, st) for i in range(3)]
    pden = p.ps("at_pden", (128, 512), F32, st)
    ppv = p.ps("at_ppv", (128, 512), F32, st)

    kvq = p.dq("s5", 8)
    kvt0 = kvq.k
    se0 = p.n("se")
    pe0 = p.n("pe")
    ve0 = p.n("ve")

    sc_marks = {}
    dp_marks = {}
    dpk2_marks = {}
    prd_marks = {}
    exp_idx = {}
    nexp = [0]

    with nc.Block() as blk:
        @blk.tensor
        def _(e):
            def denpv_pair(h, qt, kc2):
                if kc2 == 0:
                    e.wait_ge(p.sem("ve"), ve0 + 2 * (4 * h + qt))
                e.wait_ge(p.sem("se"), se0 + (h * 4 + qt) * 16 + kc2 + 1)
                for half in range(2):
                    kc = 2 * kc2 + half
                    e.matmul(pden[0:1, :], g["ones_col"][:, 0:1],
                             probs[qt % 2][:, kc * 512:(kc + 1) * 512],
                             start=(kc == 0), stop=(kc == 31))
                    mm = e.matmul(ppv[:, :], vh[h % 2][:, kc * 128:(kc + 1) * 128],
                                  probs[qt % 2][:, kc * 512:(kc + 1) * 512],
                                  start=(kc == 0), stop=(kc == 31))
                dpk2_marks[(4 * h + qt, kc2)] = p.inc(mm, "pe", 1)

            def prd(h, qt):
                dp_marks[(h, qt)] = dpk2_marks[(4 * h + qt, 15)]
                e.wait_ge(p.sem("ve"), ve0 + 2 * (4 * h + qt) + 1)
                prd_marks[(h, qt)] = p.inc(
                    e.matmul(pden[:, :], g["ones_row_f32"][0:1, :], rdrow[0:1, :],
                             start=True, stop=True), "pe", 1)

            def score_pair(h, qt, kc2):
                gk2 = 16 * (4 * h + qt) + kc2
                if gk2 >= 3:
                    e.wait_ge(p.sem("se"), se0 + gk2 - 2)
                for half in range(2):
                    kc = 2 * kc2 + half
                    mm = e.matmul(psc[gk2 % 3][:, half * 512:(half + 1) * 512],
                                  kh[h % 2][:, kc * 128:(kc + 1) * 128],
                                  attnout[:, h * 2048 + qt * 512:h * 2048 + (qt + 1) * 512],
                                  start=True, stop=True)
                sc_marks[(h, qt, kc2)] = p.inc(mm, "pe", 1)

            for h in range(16):
                for t4 in range(4):
                    kvq.wait(e, kvt0 + 4 * h + t4 + 1)
                for kc2 in range(16):
                    score_pair(h, 0, kc2)
                for qt in range(1, 4):
                    for kc2 in range(16):
                        score_pair(h, qt, kc2)
                        denpv_pair(h, qt - 1, kc2)
                    prd(h, qt - 1)
                for kc2 in range(16):
                    denpv_pair(h, 3, kc2)
                prd(h, 3)

        @blk.sync
        def _(e):
            for h in range(16):
                if h >= 2:
                    e.wait_ge(p.sem("pe"), dp_marks[(h - 2, 3)])
                kvq.issue(e, kh[h % 2][:, 0:2048], ag_out[h * 128:(h + 1) * 128, :])
                kvq.issue(e, kh[h % 2][:, 2048:4096], ag_out[4096 + h * 128:4096 + (h + 1) * 128, :])
                kvq.issue(e, vh[h % 2][:, 0:2048].rearrange("p (kc d) -> p kc d", kc=16),
                          ag_out[2048:4096, h * 128:(h + 1) * 128]
                          .rearrange("(kc q) d -> q kc d", q=128))
                kvq.issue(e, vh[h % 2][:, 2048:4096].rearrange("p (kc d) -> p kc d", kc=16),
                          ag_out[4096 + 2048:4096 + 4096, h * 128:(h + 1) * 128]
                          .rearrange("(kc q) d -> q kc d", q=128))

        @blk.scalar
        def _(e):
            for h in range(16):
                for qt in range(4):
                    for kc2 in range(16):
                        e.wait_ge(p.sem("pe"), sc_marks[(h, qt, kc2)])
                        gq = 4 * h + qt
                        gq2 = 16 * gq + kc2
                        if gq >= 1:
                            e.wait_ge(p.sem("pe"), dpk2_marks[(gq - 1, kc2)])
                        p.inc(e.activation(probs[qt % 2][:, kc2 * 1024:(kc2 + 1) * 1024],
                                           psc[gq2 % 3][:, :], AF.Exp,
                                           scale=g["qscale_col"][:, 0:1]), "se", 1)

        @blk.vector
        def _(e):
            for h in range(16):
                for qt in range(4):
                    e.wait_ge(p.sem("pe"), dp_marks[(h, qt)])
                    p.inc(e.reciprocal(rdrow[0:1, :], pden[0:1, :]), "ve", 1)
                    e.wait_ge(p.sem("pe"), prd_marks[(h, qt)])
                    r = e.tensor_copy(rdb[:, :], pden[:, :])
                    p.edge(e, r, "vch")
                    p.inc(e.tensor_mul(attnout[:, h * 2048 + qt * 512:h * 2048 + (qt + 1) * 512],
                                       ppv[:, :], rdb[:, :]), "ve", 1)
    st.close()
    p.barrier()



def emit_l1final(p, g, attnout, g1_dram, x1_dram, out_dram):
    nc = p.nc
    st = ExitStack()
    wp = [p.sb(f"fn_wp{i}", (128, NDC * 128), BF16, st) for i in range(2)]
    bpr = p.sb("fn_bpr", (1, 2048), BF16, st)
    s1f = p.sb("fn_s1f", (128, 512), BF16, st)
    s1tok = [p.sb(f"fn_s1t{i}", (128, 128), BF16, st) for i in range(2)]
    g1t = [p.sb(f"fn_g1t{i}", (128, 128), BF16, st) for i in range(4)]
    x1t = [p.sb(f"fn_x1t{i}", (128, 128), F32, st) for i in range(4)]
    x2t = [p.sb(f"fn_x2t{i}", (128, 128), F32, st) for i in range(2)]
    sc = p.sb("fn_sc", (128, 128), F32, st)

    wpq = p.dq("g1", 8)
    ldq = p.dq("s4", 8)
    o2q = p.dq("s3", 4)
    cq = p.dq("g2", 4)
    with nc.Block() as blk:
        @blk.gpsimd
        def _(e):
            cq.issue(e, bpr[:, :], g["d_bp_rows"][:, :])
            cq.wait_all(e)
    p.barrier()

    for oc in range(16):
        stO = ExitStack()
        ps1 = [p.ps(f"fn_ps1{oc}_{i}", (128, 512), F32, stO) for i in range(2)]
        pxT = [p.ps(f"fn_pxT{oc}_{i}", (128, 128), BF16, stO) for i in range(2)]
        se0 = p.n("se")
        pe0 = p.n("pe")
        ve0 = p.n("ve")
        wt0 = wpq.k
        lt0 = ldq.k
        ot0 = o2q.k
        with nc.Block() as blk:
            @blk.gpsimd
            def _(e):
                wpq.issue(e, wp[oc % 2][:, :].rearrange("p (hc o) -> p hc o", hc=NDC),
                          g["d_wp_sw"][oc].rearrange("hc p o -> p hc o"))

            @blk.tensor
            def _(e):
                wpq.wait(e, wt0 + 1)
                for ts in range(4):
                    if ts >= 2:
                        e.wait_ge(p.sem("se"), se0 + ts - 1)
                    for hc in range(NDC):
                        e.matmul(ps1[ts % 2][:, :], wp[oc % 2][:, hc * 128:(hc + 1) * 128],
                                 attnout[:, hc * 2048 + ts * 512:hc * 2048 + (ts + 1) * 512],
                                 start=(hc == 0), stop=False)
                    mm = e.matmul(ps1[ts % 2][:, :], bpr[0:1, oc * 128:(oc + 1) * 128], g["ones_bf"][0:1, :],
                                  start=False, stop=True)
                    p.inc(mm, "pe", 1)
                    for t4 in range(4):
                        k4 = ts * 4 + t4
                        e.wait_ge(p.sem("se"), se0 + ts + 1)
                        if k4 >= 2:
                            e.wait_ge(p.sem("ve"), ve0 + 3 * (k4 - 1))
                        p.inc(e.transpose(pxT[k4 % 2][:, :], s1f[:, t4 * 128:(t4 + 1) * 128],
                                          g["ident_bf"][:, :]), "pe", 1)

            @blk.scalar
            def _(e):
                for ts in range(4):
                    e.wait_ge(p.sem("pe"), pe0 + 5 * ts + 1)
                    if ts >= 1:
                        e.wait_ge(p.sem("pe"), pe0 + 5 * ts)
                    p.inc(e.activation(s1f[:, :], ps1[ts % 2][:, :], AF.Copy), "se", 1)

            @blk.sync
            def _(e):
                for tc in range(4):
                    ldq.issue(e, g1t[tc][:, :], g1_dram[tc * 128:(tc + 1) * 128, oc * 128:(oc + 1) * 128])
                    ldq.issue(e, x1t[tc][:, :], x1_dram[tc * 128:(tc + 1) * 128, oc * 128:(oc + 1) * 128])
                for tc in range(NTC):
                    e.wait_ge(p.sem("ve"), ve0 + 3 * (tc + 1))
                    o2q.issue(e, out_dram[tc * 128:(tc + 1) * 128, oc * 128:(oc + 1) * 128],
                              x2t[tc % 2][:, :])
                    if tc + 4 < NTC:
                        ldq.issue(e, g1t[tc % 4][:, :],
                                  g1_dram[(tc + 4) * 128:(tc + 5) * 128, oc * 128:(oc + 1) * 128])
                        ldq.issue(e, x1t[tc % 4][:, :],
                                  x1_dram[(tc + 4) * 128:(tc + 5) * 128, oc * 128:(oc + 1) * 128])

            @blk.vector
            def _(e):
                for tc in range(NTC):
                    k4 = tc
                    e.wait_ge(p.sem("pe"), pe0 + (tc // 4) * 5 + 1 + (tc % 4) + 1)
                    r = e.tensor_copy(s1tok[k4 % 2][:, :], pxT[k4 % 2][:, :])
                    p.edge(e, r, "vch")
                    ldq.wait(e, lt0 + 2 * tc + 1)
                    ldq.wait(e, lt0 + 2 * tc + 2)
                    r = e.tensor_mul(sc[:, :], s1tok[k4 % 2][:, :], g1t[tc % 4][:, :])
                    p.edge(e, r, "vch")
                    if ot0 + tc - 1 >= 1:
                        o2q.wait(e, ot0 + tc - 1)
                    r = e.tensor_add(x2t[tc % 2][:, :], sc[:, :], x1t[tc % 4][:, :])
                    p.inc(r, "ve", 3)
        stO.close()
        p.barrier()
    with nc.Block() as blk:
        @blk.sync
        def _(e):
            o2q.wait_all(e)
    p.barrier()
    st.close()



def build_full():
    p = Prog()
    g = {}
    for nm, shape, dt in [
        ("d_ident", (128, 128), BF16), ("d_ident_f32", (128, 128), F32),
        ("d_WkT16", (16, 128, 512), F32), ("d_WvT16", (16, 128, 512), F32),
        ("d_wmb_hsw", (16, 16, 128, 128), F32), ("d_bk_row", (1, 512), F32),
        ("d_bv_row", (1, 512), F32), ("d_bmb_row", (1, 2048), F32),
        ("d_mem_aug", (16, 768, 129), F32),
        ("d_wg_sw", (64, 128, 2048), I8), ("d_wu_sw", (64, 128, 2048), I8),
        ("d_wd_sw", (2, 64, 128, 1024), I8), ("d_sg_sw", (128, 64), F32),
        ("d_su_sw", (128, 64), F32), ("d_sdown_bc", (128, 2048), F32),
        ("d_wq_hsw", (16, 16, 128, 128), F32), ("d_wg_gsw", (4, 16, 128, 512), F32),
        ("d_bq_cols", (128, 16), F32), ("d_bg_row", (1, 2048), F32),
        ("d_wk1_hsw", (16, 16, 128, 128), F32), ("d_wv1_sw", (16, 128, 2048), F32),
        ("d_bk1_rows", (1, 2048), F32), ("d_bv1_row", (1, 2048), F32),
        ("d_wg1_sw", (44, 128, 2048), I32), ("d_wu1_sw", (44, 128, 2048), I32),
        ("d_wd1_sw", (4, 44, 128, 512), I32), ("d_sg1_sw", (128, 44), F32),
        ("d_su1_sw", (128, 44), F32), ("d_sd1_bc", (128, 2048), F32),
        ("d_wq1_hsw", (16, 16, 128, 128), F32), ("d_bq1_rows", (1, 2048), F32),
        ("d_wp_sw", (16, 16, 128, 128), F32), ("d_bp_rows", (1, 2048), F32),
    ]:
        g[nm] = p.dram(nm, shape, dt, kind="ExternalInput").ap()
    d_x0 = p.dram("d_x0", (T, DM), F32, kind="ExternalInput").ap()
    o_out = p.dram("o_out", (T, DM), F32, kind="ExternalOutput").ap()

    ar_in = p.dram("ar_in", (12384, 128), F32).ap()
    ar_out = p.dram("ar_out", (12384, 128), F32).ap()
    g0scr = p.dram("g0scr", (T, DM), BF16).ap()
    x2d = p.dram("x2d", (512, DM), BF16).ap()
    x1scr = p.dram("x1scr", (T, DM), F32).ap()
    g1scr = p.dram("g1scr", (T, DM), BF16).ap()
    ag_in = p.dram("ag_in", (4096, 2048), BF16).ap()
    ag_out = p.dram("ag_out", (8192, 2048), BF16).ap()

    xn_stk = ExitStack()
    xnT = xn_stk.enter_context(p.nc.sbuf_tensor("xnT", [128, 16 * T], BF16))
    emit_const_setup(p, g)
    emit_ln(p, d_x0, xnT, g, "ln0")
    emit_l0p1(p, xnT, g, ar_in, ar_out)
    emit_ar(p, ar_in, ar_out)
    emit_ffn0(p, xnT, g, g0scr)
    nm_st = ExitStack()
    nm_aug = nm_st.enter_context(p.nc.sbuf_tensor("nm_aug", [128, 96 * 129], BF16))
    emit_newmem(p, g, ar_out, nm_aug)
    emit_l0p2(p, xnT, g, nm_aug, x2d, g0scr, d_x0, x1scr)
    nm_st.close()
    emit_ln(p, x1scr, xnT, g, "ln1")
    emit_l1kv(p, xnT, g, ag_in, ag_out)
    emit_ffn1(p, xnT, g, g1scr)
    at_st = ExitStack()
    attnout = at_st.enter_context(p.nc.sbuf_tensor("attnout", [128, 16 * T], BF16))
    emit_l1q(p, xnT, g, attnout)
    emit_l1attn(p, g, ag_out, attnout)
    emit_l1final(p, g, attnout, g1scr, x1scr, o_out)
    at_st.close()
    return p.finish()


def make_inmaps(h):
    maps = []
    for c in range(NCORES):
        maps.append({
            "d_ident": h["ident"], "d_ident_f32": h["ident_f32"], "d_x0": h["x0_cores"][c],
            "d_WkT16": h["WkT16"], "d_WvT16": h["WvT16"], "d_wmb_hsw": h["wmb_hsw"],
            "d_bk_row": h["bk_row"], "d_bv_row": h["bv_row"], "d_bmb_row": h["bmb_row"],
            "d_mem_aug": h["mem_aug"],
            "d_wg_sw": h["wg0_sw"], "d_wu_sw": h["wu0_sw"], "d_wd_sw": h["wd0_sw"],
            "d_sg_sw": h["sg0_sw"], "d_su_sw": h["su0_sw"], "d_sdown_bc": h["sd0_bc"],
            "d_wq_hsw": h["wq_hsw"], "d_wg_gsw": h["wg_gsw"],
            "d_bq_cols": h["bq_cols"], "d_bg_row": h["bg_row"],
            "d_wk1_hsw": h["wk1_hsw"], "d_wv1_sw": h["wv1_sw"],
            "d_bk1_rows": h["bk1_rows"], "d_bv1_row": h["bv1_row"],
            "d_wg1_sw": h["wg1_sw"], "d_wu1_sw": h["wu1_sw"], "d_wd1_sw": h["wd1_sw"],
            "d_sg1_sw": h["sg1_sw"], "d_su1_sw": h["su1_sw"], "d_sd1_bc": h["sd1_bc"],
            "d_wq1_hsw": h["wq1_hsw"], "d_bq1_rows": h["bq1_rows"],
            "d_wp_sw": h["wp_sw"], "d_bp_rows": h["bp_rows"],
        })
    return maps


def kernel(**inputs):
    h = hostprep(inputs)
    nc = build_full()
    res = run_bass_kernel_spmd(nc, make_inmaps(h), core_ids=list(range(NCORES)))
    out = np.empty((4, 4096, 2048), dtype=np.float32)
    for c in range(NCORES):
        out[c // 2, (c % 2) * T:(c % 2 + 1) * T] = np.asarray(res.results[c]["o_out"])
    return out


# version 08
# speedup vs baseline: 1.0617x; 1.0617x over previous
import numpy as np
import ml_dtypes
from contextlib import ExitStack

import concourse.bass as bass
import concourse.mybir as mybir
from concourse.bass_utils import run_bass_kernel_spmd

F32 = mybir.dt.float32
BF16 = mybir.dt.bfloat16
I8 = mybir.dt.int8
I32 = mybir.dt.int32
AF = mybir.ActivationFunctionType
ALU = mybir.AluOpType
BF = ml_dtypes.bfloat16

NCORES = 8
T = 2048
NTC = T // 128
DM = 2048
NDC = DM // 128
FFN0_I = 8192
FFN1_I = 5632
EPSLN = 1e-5
EPS = 1e-8



class Prog:
    def __init__(self):
        self.nc = bass.Bass("TRN2", target_bir_lowering=False)
        self.es = ExitStack()
        self._sems = {}
        self.cnt = {}
        self.bar_k = 0

    def sem(self, name):
        if name not in self._sems:
            self._sems[name] = self.es.enter_context(self.nc.semaphore(name))
            self.cnt[name] = 0
        return self._sems[name]

    def inc(self, instr, name, k=1):
        s = self.sem(name)
        instr.then_inc(s, k)
        self.cnt[name] += k
        return self.cnt[name]

    def n(self, name):
        self.sem(name)
        return self.cnt[name]

    def sb(self, name, shape, dtype, stack=None):
        return (stack or self.es).enter_context(self.nc.sbuf_tensor(name, list(shape), dtype))

    def ps(self, name, shape, dtype=F32, stack=None):
        return (stack or self.es).enter_context(self.nc.psum_tensor(name, list(shape), dtype))

    def dram(self, name, shape, dtype, kind="Internal"):
        return self.nc.dram_tensor(name, list(shape), dtype, kind=kind)

    def barrier(self):
        bar = self.sem("bar")
        self.bar_k += 1
        tgt = 5 * self.bar_k
        with self.nc.Block() as blk:
            for engname in ("tensor", "vector", "scalar", "gpsimd", "sync"):
                def mk(e, _t=tgt):
                    e.drain().then_inc(bar, 1)
                    e.wait_ge(bar, _t)
                getattr(blk, engname)(mk)
        self.cnt["bar"] = tgt

    def finish(self):
        self.es.close()
        return self.nc

    def edge(self, e, instr, nm):
        c = self.inc(instr, nm)
        e.wait_ge(self.sem(nm), c)

    def dq(self, name, depth=2):
        key = (name, depth)
        if not hasattr(self, "_dqs"):
            self._dqs = {}
        if key not in self._dqs:
            self._dqs[key] = DQ(self, name, depth)
        return self._dqs[key]


class DQ:

    def __init__(self, p, name, depth):
        self.p, self.name, self.depth = p, name, depth
        self.k = 0

    def issue(self, e, out, in_, **kw):
        s = f"{self.name}{self.k % self.depth}"
        self.p.inc(e.dma_start(out=out, in_=in_, **kw), s, 16)
        self.k += 1
        return self.k

    def wait(self, e, ticket):
        i = ticket - 1
        s = f"{self.name}{i % self.depth}"
        e.wait_ge(self.p.sem(s), 16 * (i // self.depth + 1))

    def wait_all(self, e):
        for d in range(self.depth):
            n = (self.k + self.depth - 1 - d) // self.depth
            if n:
                e.wait_ge(self.p.sem(f"{self.name}{d}"), 16 * n)



def emit_const_setup(p, g):
    nc = p.nc
    g["ident_bf"] = p.sb("ident_bf", (128, 128), BF16)
    g["ident_f32"] = p.sb("ident_f32", (128, 128), F32)
    g["ones_bf"] = p.sb("ones_bf", (1, 512), BF16)
    g["epsln_col"] = p.sb("epsln_col", (128, 1), F32)
    g["neg1_col"] = p.sb("neg1_col", (128, 1), F32)
    g["ones_col"] = p.sb("ones_col", (128, 1), BF16)
    g["ones_row_f32"] = p.sb("ones_row_f32", (1, 128), F32)
    g["qscale_col"] = p.sb("qscale_col", (128, 1), F32)
    cst = p.dq("s2", 4)
    with nc.Block() as blk:
        @blk.sync
        def _(e):
            cst.issue(e, g["ident_bf"][:, :], g["d_ident"][:, :])
            cst.issue(e, g["ident_f32"][:, :], g["d_ident_f32"][:, :])
            cst.wait_all(e)

        @blk.vector
        def _(e):
            p.inc(e.memset(g["ones_bf"][:, :], 1.0), "ve", 1)
            p.inc(e.memset(g["epsln_col"][:, :], EPSLN), "ve", 1)
            p.inc(e.memset(g["neg1_col"][:, :], -1.0), "ve", 1)
            p.inc(e.memset(g["ones_col"][:, :], 1.0), "ve", 1)
            p.inc(e.memset(g["ones_row_f32"][:, :], 1.0), "ve", 1)
            p.inc(e.memset(g["qscale_col"][:, :], 0.08838834764831845), "ve", 1)
    p.barrier()



def emit_ln(p, x_dram, xnT, g, tag):
    nc = p.nc
    st = ExitStack()
    xdt = F32
    xt = [p.sb(f"{tag}_xt{i}", (128, DM), xdt, st) for i in range(2)]
    xn = [p.sb(f"{tag}_xn{i}", (128, DM), BF16, st) for i in range(2)]
    st6 = p.sb(f"{tag}_st6", (128, 24), F32, st)
    mv = p.sb(f"{tag}_mv", (128, 2), F32, st)
    sq = p.sb(f"{tag}_sq", (128, 1), F32, st)
    rstd = p.sb(f"{tag}_rstd", (128, 1), F32, st)
    b2 = p.sb(f"{tag}_b2", (128, 1), F32, st)
    pT = [p.ps(f"{tag}_pT{i}", (128, 128), BF16, st) for i in range(2)]

    vs0 = p.n("vs")
    se0 = p.n("se")
    ve0 = p.n("ve")
    ev0 = p.n("ev")
    pe0 = p.n("pe")
    lnx = p.dq("s1", 4)
    tk0 = lnx.k

    with nc.Block() as blk:
        @blk.sync
        def _(e):
            for i in range(NTC):
                if i >= 2:
                    e.wait_ge(p.sem("ve"), ve0 + i - 1)
                lnx.issue(e, xt[i % 2][:, :], x_dram[i * 128:(i + 1) * 128, :])

        @blk.scalar
        def _(e):
            for i in range(NTC):
                e.wait_ge(p.sem("vs"), vs0 + i + 1)
                p.inc(e.activation(sq[:, :], mv[:, 1:2], AF.Sqrt, bias=g["epsln_col"][:, 0:1]), "se", 1)

        @blk.vector
        def _(e):
            for i in range(NTC):
                x = xt[i % 2]
                lnx.wait(e, tk0 + i + 1)
                for j in range(4):
                    e.bn_stats(st6[:, 6 * j:6 * j + 6], x[:, 512 * j:512 * (j + 1)])
                p.edge(e, e.drain(), "vch")
                p.inc(e.bn_aggr(mv[:, :], st6[:, :]), "vs", 1)
                e.wait_ge(p.sem("vs"), vs0 + i + 1)
                e.wait_ge(p.sem("se"), se0 + i + 1)
                r = e.reciprocal(rstd[:, :], sq[:, :])
                p.edge(e, r, "vch")
                b = e.scalar_tensor_tensor(b2[:, :], mv[:, 0:1], -1.0, rstd[:, :], ALU.mult, ALU.mult)
                p.edge(e, b, "vch")
                if i >= 2:
                    e.wait_ge(p.sem("pe"), pe0 + (i - 1) * NDC)
                p.inc(e.tensor_scalar(xn[i % 2][:, :], x[:, :], rstd[:, 0:1], b2[:, 0:1],
                                      ALU.mult, ALU.add), "ve", 1)
                for dc in range(NDC):
                    k = i * NDC + dc
                    e.wait_ge(p.sem("pe"), pe0 + k + 1)
                    p.inc(e.tensor_copy(xnT[:, dc * T + i * 128:dc * T + (i + 1) * 128],
                                        pT[k % 2][:, :]), "ev", 1)

        @blk.tensor
        def _(e):
            for i in range(NTC):
                e.wait_ge(p.sem("ve"), ve0 + i + 1)
                for dc in range(NDC):
                    k = i * NDC + dc
                    if k >= 2:
                        e.wait_ge(p.sem("ev"), ev0 + k - 1)
                    p.inc(e.transpose(pT[k % 2][:, :], xn[i % 2][:, dc * 128:(dc + 1) * 128],
                                      g["ident_bf"][:, :]), "pe", 1)
    st.close()
    p.barrier()



def emit_ffn0(p, xnT, g, g0_dram):
    nc = p.nc
    NIC = FFN0_I // 128
    st = ExitStack()
    wg = [p.sb(f"f0_wg{i}", (128, DM), BF16, st) for i in range(4)]
    wu = [p.sb(f"f0_wu{i}", (128, DM), BF16, st) for i in range(4)]
    wd = [p.sb(f"f0_wd{i}", (128, 1024), BF16, st) for i in range(4)]
    sg = p.sb("f0_sg", (128, NIC), F32, st)
    su = p.sb("f0_su", (128, NIC), F32, st)
    sdb = p.sb("f0_sdb", (128, DM), F32, st)
    fuse = p.sb("f0_fuse", (128, NIC * 512), BF16, st)
    gsc = [p.sb(f"f0_gsc{i}", (128, 512), BF16, st) for i in range(2)]
    usc = [p.sb(f"f0_usc{i}", (128, 512), BF16, st) for i in range(2)]
    ysc = [p.sb(f"f0_ysc{i}", (128, 512), F32, st) for i in range(2)]
    g0t = [p.sb(f"f0_g0t{i}", (128, 512), BF16, st) for i in range(2)]

    sc0 = p.dq("s2", 4)
    with nc.Block() as blk:
        @blk.sync
        def _(e):
            sc0.issue(e, sg[:, :], g["d_sg_sw"][:, :])
            sc0.issue(e, su[:, :], g["d_su_sw"][:, :])
            sc0.issue(e, sdb[:, :], g["d_sdown_bc"][:, :])
            sc0.wait_all(e)
    p.barrier()

    for tq in range(4):
        t0 = tq * 512
        stq = ExitStack()
        pg = [p.ps(f"f0_pg{tq}_{i}", (128, 512), F32, stq) for i in range(2)]
        pu = [p.ps(f"f0_pu{tq}_{i}", (128, 512), F32, stq) for i in range(2)]
        se0 = p.n("se")
        ve0 = p.n("ve")
        pe0 = p.n("pe")
        wq = p.dq("g1", 8)
        wtk0 = wq.k
        with nc.Block() as blk:
            @blk.gpsimd
            def _(e):
                for ic in range(NIC):
                    if ic >= 4:
                        e.wait_ge(p.sem("pe"), pe0 + 2 * (ic - 3))
                    wq.issue(e, wg[ic % 4][:, :], g["d_wg_sw"][ic])
                    wq.issue(e, wu[ic % 4][:, :], g["d_wu_sw"][ic])

            @blk.tensor
            def _(e):
                for ic in range(NIC):
                    wq.wait(e, wtk0 + 2 * ic + 1)
                    wq.wait(e, wtk0 + 2 * ic + 2)
                    if ic >= 2:
                        e.wait_ge(p.sem("se"), se0 + 2 * (ic - 1))
                    for dc in range(NDC):
                        mm = e.matmul(pg[ic % 2][:, :], wg[ic % 4][:, dc * 128:(dc + 1) * 128],
                                      xnT[:, dc * T + t0:dc * T + t0 + 512],
                                      start=(dc == 0), stop=(dc == NDC - 1))
                    p.inc(mm, "pe", 1)
                    for dc in range(NDC):
                        mm = e.matmul(pu[ic % 2][:, :], wu[ic % 4][:, dc * 128:(dc + 1) * 128],
                                      xnT[:, dc * T + t0:dc * T + t0 + 512],
                                      start=(dc == 0), stop=(dc == NDC - 1))
                    p.inc(mm, "pe", 1)

            @blk.scalar
            def _(e):
                for ic in range(NIC):
                    e.wait_ge(p.sem("pe"), pe0 + 2 * ic + 1)
                    if ic >= 2:
                        e.wait_ge(p.sem("ve"), ve0 + ic - 1)
                    p.inc(e.activation(gsc[ic % 2][:, :], pg[ic % 2][:, :], AF.Silu,
                                       scale=sg[:, ic:ic + 1]), "se", 1)
                    e.wait_ge(p.sem("pe"), pe0 + 2 * ic + 2)
                    p.inc(e.activation(usc[ic % 2][:, :], pu[ic % 2][:, :], AF.Copy,
                                       scale=su[:, ic:ic + 1]), "se", 1)

            @blk.vector
            def _(e):
                for ic in range(NIC):
                    e.wait_ge(p.sem("se"), se0 + 2 * (ic + 1))
                    p.inc(e.tensor_mul(fuse[:, ic * 512:(ic + 1) * 512],
                                       gsc[ic % 2][:, :], usc[ic % 2][:, :]), "ve", 1)
        stq.close()
        p.barrier()

        for oh in range(2):
            std = ExitStack()
            pd = [[p.ps(f"f0_pd{tq}_{oh}_{t4}_{ob}", (128, 512), F32, std)
                   for ob in range(2)] for t4 in range(4)]
            pe0 = p.n("pe")
            ve0 = p.n("ve")
            se0 = p.n("se")
            wdq = p.dq("g1", 8)
            dtk0 = wdq.k
            gsq = p.dq("s3", 4)
            stk0 = gsq.k
            with nc.Block() as blk:
                @blk.gpsimd
                def _(e):
                    for ic in range(NIC):
                        if ic >= 4:
                            e.wait_ge(p.sem("pe"), pe0 + (ic - 3))
                        wdq.issue(e, wd[ic % 4][:, :], g["d_wd_sw"][oh, ic])

                @blk.tensor
                def _(e):
                    for ic in range(NIC):
                        wdq.wait(e, dtk0 + ic + 1)
                        for t4 in range(4):
                            for ob in range(2):
                                mm = e.matmul(pd[t4][ob][:, :],
                                              fuse[:, ic * 512 + t4 * 128:ic * 512 + (t4 + 1) * 128],
                                              wd[ic % 4][:, ob * 512:(ob + 1) * 512],
                                              start=(ic == 0), stop=(ic == NIC - 1))
                        p.inc(mm, "pe", 1)

                @blk.vector
                def _(e):
                    e.wait_ge(p.sem("pe"), pe0 + NIC)
                    for j in range(8):
                        t4, ob = j // 2, j % 2
                        ocol = oh * 1024 + ob * 512
                        if j >= 2:
                            e.wait_ge(p.sem("se"), se0 + j - 1)
                        p.inc(e.tensor_mul(ysc[j % 2][:, :], pd[t4][ob][:, :],
                                           sdb[:, ocol:ocol + 512]), "ve", 1)

                @blk.scalar
                def _(e):
                    for j in range(8):
                        e.wait_ge(p.sem("ve"), ve0 + j + 1)
                        if stk0 + j - 1 >= 1:
                            gsq.wait(e, stk0 + j - 1)
                        p.inc(e.activation(g0t[j % 2][:, :], ysc[j % 2][:, :], AF.Silu), "se", 1)

                @blk.sync
                def _(e):
                    for j in range(8):
                        t4, ob = j // 2, j % 2
                        ocol = oh * 1024 + ob * 512
                        e.wait_ge(p.sem("se"), se0 + j + 1)
                        gsq.issue(e, g0_dram[t0 + t4 * 128:t0 + (t4 + 1) * 128,
                                             ocol:ocol + 512], g0t[j % 2][:, :])
            std.close()
            p.barrier()
    with nc.Block() as blk:
        @blk.sync
        def _(e):
            p.dq("s3", 4).wait_all(e)
    p.barrier()
    st.close()



def emit_dump_xnT(p, xnT, dram, tag="dxnT"):
    q = p.dq("s1", 4)
    with p.nc.Block() as blk:
        @blk.sync
        def _(e):
            for dc in range(NDC):
                q.issue(e, dram[dc], xnT[:, dc * T:(dc + 1) * T])
            q.wait_all(e)
    p.barrier()


def emit_load_xnT(p, xnT, dram, tag="lxnT"):
    q = p.dq("s1", 4)
    with p.nc.Block() as blk:
        @blk.sync
        def _(e):
            for dc in range(NDC):
                q.issue(e, xnT[:, dc * T:(dc + 1) * T], dram[dc])
            q.wait_all(e)
    p.barrier()



def hostprep(inputs):
    h = {}
    C = np.ascontiguousarray
    h["WqT"] = C(inputs["L0_attn_Wq"].T)
    h["WkT"] = C(inputs["L0_attn_Wk"].T)
    h["WvT"] = C(inputs["L0_attn_Wv"].T)
    h["WgT"] = C(inputs["L0_attn_Wg"].T)
    h["WmbT"] = C(inputs["L0_attn_Wmb"].T)
    for nm in ("bq", "bk", "bv", "bg", "bmb"):
        b = inputs[f"L0_attn_{nm}"]
        h[f"{nm}_row"] = C(b.reshape(1, -1))
        h[f"{nm}_col"] = C(b.reshape(-1, 1))
    mem = inputs["L0_attn_memory"]
    nrm = inputs["L0_attn_norm"]
    h["mem_aug"] = C(np.concatenate([mem, nrm[:, :, None]], axis=2))
    h["WkT16"] = C(h["WkT"].reshape(16, 128, 512))
    h["WvT16"] = C(h["WvT"].reshape(16, 128, 512))
    h["wmb_hsw"] = C(h["WmbT"].reshape(16, 128, 16, 128).transpose(2, 0, 1, 3))
    h["wq_hsw"] = C(h["WqT"].reshape(16, 128, 16, 128).transpose(2, 0, 1, 3))
    h["wg_gsw"] = C(h["WgT"].reshape(16, 128, 4, 512).transpose(2, 0, 1, 3))
    h["bq_cols"] = C(inputs["L0_attn_bq"].reshape(16, 128).T)
    wg8 = inputs["L0_ffn_w_gate"]
    wu8 = inputs["L0_ffn_w_up"]
    wd8 = inputs["L0_ffn_w_down"]
    h["wg0_sw"] = C(wg8.reshape(16, 128, 64, 128).transpose(2, 1, 0, 3).reshape(64, 128, 2048))
    h["wu0_sw"] = C(wu8.reshape(16, 128, 64, 128).transpose(2, 1, 0, 3).reshape(64, 128, 2048))
    h["wd0_sw"] = C(wd8.reshape(64, 128, 2, 1024).transpose(2, 0, 1, 3))
    h["sg0_sw"] = C(inputs["L0_ffn_s_gate"].reshape(64, 128).T)
    h["su0_sw"] = C(inputs["L0_ffn_s_up"].reshape(64, 128).T)
    h["sd0_bc"] = C(np.broadcast_to(inputs["L0_ffn_s_down"].reshape(1, 2048), (128, 2048)))
    h["Wqkv"] = C(inputs["L1_attn_W_qkv"])
    bqkv = inputs["L1_attn_b_qkv"]
    h["bqkv_col"] = C(bqkv.reshape(-1, 1))
    h["bqkv_row"] = C(bqkv.reshape(1, -1))
    h["Wproj"] = C(inputs["L1_attn_W_proj"])
    h["bproj_col"] = C(inputs["L1_attn_b_proj"].reshape(-1, 1))
    Wq1 = inputs["L1_attn_W_qkv"][:, 0:2048]
    Wk1 = inputs["L1_attn_W_qkv"][:, 2048:4096]
    Wv1 = inputs["L1_attn_W_qkv"][:, 4096:6144]
    h["wq1_hsw"] = C(Wq1.reshape(16, 128, 16, 128).transpose(2, 0, 1, 3))
    h["wk1_hsw"] = C(Wk1.reshape(16, 128, 16, 128).transpose(2, 0, 1, 3))
    h["wv1_sw"] = C(Wv1.reshape(16, 128, 2048))
    h["bq1_rows"] = C(bqkv[0:2048].reshape(1, 2048))
    h["bk1_rows"] = C(bqkv[2048:4096].reshape(1, 2048))
    h["bv1_row"] = C(bqkv[4096:6144].reshape(1, 2048))
    h["wp_sw"] = C(inputs["L1_attn_W_proj"].reshape(16, 128, 16, 128).transpose(2, 0, 1, 3))
    h["bp_rows"] = C(inputs["L1_attn_b_proj"].reshape(1, 2048))
    wqg = inputs["L1_ffn_wq_gate"]
    wqu = inputs["L1_ffn_wq_up"]
    wqd = inputs["L1_ffn_wq_down"]
    wgT = wqg.T
    wuT = wqu.T
    h["wg1_sw"] = C(wgT.reshape(16, 128, 44, 128).transpose(2, 1, 0, 3).reshape(44, 128, 5632 * 0 + 2048))
    h["wu1_sw"] = C(wuT.reshape(16, 128, 44, 128).transpose(2, 1, 0, 3).reshape(44, 128, 2048))
    wdT = wqd.T
    h["wd1_sw"] = C(wdT.reshape(44, 128, 4, 512).transpose(2, 0, 1, 3))
    h["sg1_sw"] = C(inputs["L1_ffn_s_gate"].reshape(44, 128).T)
    h["su1_sw"] = C(inputs["L1_ffn_s_up"].reshape(44, 128).T)
    h["sd1_bc"] = C(np.broadcast_to(inputs["L1_ffn_s_down"].reshape(1, 2048), (128, 2048)))
    h["ident"] = np.eye(128, dtype=BF)
    h["ident_f32"] = np.eye(128, dtype=np.float32)
    x0 = inputs["L0_attn_hidden_states"]
    h["x0_cores"] = [C(x0[c // 2, (c % 2) * T:(c % 2 + 1) * T]) for c in range(NCORES)]
    return h



def emit_l0p1(p, xnT, g, ar_in, ar_out):
    nc = p.nc
    st = ExitStack()
    k_all = p.sb("l0_k", (128, NTC * 512), BF16, st)
    v_all = p.sb("l0_v", (128, NTC * 512), BF16, st)
    bk_sb = p.sb("l0_bk", (1, 512), BF16, st)
    bv_sb = p.sb("l0_bv", (1, 512), BF16, st)
    bmb_sb = p.sb("l0_bmb", (1, 2048), BF16, st)

    stA = ExitStack()
    wk_sb = p.sb("l0_wk", (128, NDC * 512), BF16, stA)
    wv_sb = p.sb("l0_wv", (128, NDC * 512), BF16, stA)
    pk = [p.ps(f"l0_pk{i}", (128, 512), F32, stA) for i in range(2)]
    pv = [p.ps(f"l0_pv{i}", (128, 512), F32, stA) for i in range(2)]
    wq = p.dq("g1", 8)
    pe0 = p.n("pe")
    se0 = p.n("se")
    with nc.Block() as blk:
        @blk.gpsimd
        def _(e):
            wq.issue(e, wk_sb[:, :].rearrange("p (dc o) -> p dc o", dc=NDC), g["d_WkT16"].rearrange("dc p o -> p dc o"))
            wq.issue(e, wv_sb[:, :].rearrange("p (dc o) -> p dc o", dc=NDC), g["d_WvT16"].rearrange("dc p o -> p dc o"))
            wq.issue(e, bk_sb[:, :], g["d_bk_row"][:, :])
            wq.issue(e, bv_sb[:, :], g["d_bv_row"][:, :])
            wq.issue(e, bmb_sb[:, :], g["d_bmb_row"][:, :])

        @blk.tensor
        def _(e):
            wq.wait_all(e)
            for tc in range(NTC):
                if tc >= 2:
                    e.wait_ge(p.sem("se"), se0 + 2 * (tc - 1))
                for dc in range(NDC):
                    e.matmul(pk[tc % 2][:, :], xnT[:, dc * T + tc * 128:dc * T + (tc + 1) * 128],
                             wk_sb[:, dc * 512:(dc + 1) * 512], start=(dc == 0), stop=False)
                mm = e.matmul(pk[tc % 2][:, :], g["ones_bf"][0:1, 0:128], bk_sb[:, :],
                              start=False, stop=True)
                p.inc(mm, "pe", 1)
                for dc in range(NDC):
                    e.matmul(pv[tc % 2][:, :], xnT[:, dc * T + tc * 128:dc * T + (tc + 1) * 128],
                             wv_sb[:, dc * 512:(dc + 1) * 512], start=(dc == 0), stop=False)
                mm = e.matmul(pv[tc % 2][:, :], g["ones_bf"][0:1, 0:128], bv_sb[:, :],
                              start=False, stop=True)
                p.inc(mm, "pe", 1)

        @blk.scalar
        def _(e):
            for tc in range(NTC):
                e.wait_ge(p.sem("pe"), pe0 + 2 * tc + 1)
                p.inc(e.activation(k_all[:, tc * 512:(tc + 1) * 512], pk[tc % 2][:, :],
                                   AF.Copy), "se", 1)
                e.wait_ge(p.sem("pe"), pe0 + 2 * tc + 2)
                p.inc(e.activation(v_all[:, tc * 512:(tc + 1) * 512], pv[tc % 2][:, :],
                                   AF.Copy), "se", 1)
    stA.close()
    p.barrier()

    mk_g = p.sb("l0_mk", (128, NTC * 768), BF16, st)
    mkT_g = p.sb("l0_mkT", (128, 6 * T), BF16, st)
    wmv_g = p.sb("l0_wmv", (128, NTC * 512), BF16, st)
    wmb_h = [p.sb(f"l0_wmbh{i}", (128, NDC * 128), BF16, st) for i in range(2)]
    mb_h = p.sb("l0_mbh", (128, NTC * 128), BF16, st)
    mem_sb = [p.sb(f"l0_mem{i}", (128, 6 * 129), BF16, st) for i in range(2)]
    x2 = p.sb("l0_x2", (128, 256), BF16, st)
    dsq = p.sb("l0_dsq", (128, 768), F32, st)
    ssq_g = p.sb("l0_ssq", (128, 16), F32, st)
    rn_g = p.sb("l0_rn", (128, 16), F32, st)
    mksq_g = p.sb("l0_mksq", (128, 16), F32, st)
    rmksq_g = p.sb("l0_rmksq", (128, 16), F32, st)
    den_g = p.sb("l0_den", (128, 64), F32, st)
    dsb2 = [p.sb(f"l0_dsb{i}", (128, 512), F32, st) for i in range(2)]
    coef_g = p.sb("l0_coef", (128, 64), BF16, st)
    rden_c = p.sb("l0_rden", (128, 1), F32, st)
    scr1 = p.sb("l0_scr1", (128, 128), F32, st)
    scr2 = p.sb("l0_scr2", (128, 128), F32, st)
    dnsb = p.sb("l0_dnsb", (4, 768), F32, st)
    coef_v = coef_g[:, :].rearrange("p (tc h) -> p tc h", h=4)

    memq = p.dq("g2", 4)
    wmbq = p.dq("g1", 8)
    arq = p.dq("s4", 8)
    dnq = p.dq("s2", 4)
    art0 = arq.k
    dnt0 = dnq.k

    for gi in range(4):
        stB = ExitStack()
        pT = [p.ps(f"l0_pT{gi}_{i}", (128, 128), BF16, stB) for i in range(2)]
        se0 = p.n("se")
        ve0 = p.n("ve")
        pe0 = p.n("pe")
        ev0 = p.n("ev")
        with nc.Block() as blk:
            @blk.scalar
            def _(e):
                for tc in range(NTC):
                    if tc >= 1:
                        e.wait_ge(p.sem("ve"), ve0 + 2 * tc)
                    p.inc(e.activation(x2[:, 0:128],
                                       k_all[:, tc * 512 + gi * 128:tc * 512 + (gi + 1) * 128],
                                       AF.Relu), "se", 1)
                    p.inc(e.activation(x2[:, 128:256],
                                       k_all[:, tc * 512 + gi * 128:tc * 512 + (gi + 1) * 128],
                                       AF.Relu, scale=g["neg1_col"][:, 0:1]), "se", 1)
                e.wait_ge(p.sem("ve"), ve0 + 2 * NTC)
                p.inc(e.activation(rn_g[:, :], ssq_g[:, :], AF.Sqrt), "se", 1)

            @blk.vector
            def _(e):
                for tc in range(NTC):
                    e.wait_ge(p.sem("se"), se0 + 2 * (tc + 1))
                    o = tc * 768
                    last = None
                    for j in range(1, 4):
                        b = (j - 1) * 256
                        last = e.tensor_mul(mk_g[:, o + b + j:o + b + 256], x2[:, j:256], x2[:, 0:256 - j])
                        last = e.tensor_mul(mk_g[:, o + b:o + b + j], x2[:, 0:j], x2[:, 256 - j:256])
                    p.edge(e, e.drain(), "vch")
                    p.inc(e.nop(), "ve", 1)
                    d = e.tensor_mul(dsq[:, :], mk_g[:, o:o + 768], mk_g[:, o:o + 768])
                    p.edge(e, d, "vch")
                    r = e.tensor_reduce(ssq_g[:, tc:tc + 1], dsq[:, :], mybir.AxisListType.X, ALU.add)
                    p.inc(r, "ve", 1)
                e.wait_ge(p.sem("se"), se0 + 2 * NTC + 1)
                r = e.tensor_scalar_max(rn_g[:, :], rn_g[:, :], 1e-12)
                p.edge(e, r, "vch")
                r = e.reciprocal(rn_g[:, :], rn_g[:, :])
                p.edge(e, r, "vch")
                r = e.tensor_mul(mksq_g[:, :], ssq_g[:, :], rn_g[:, :])
                p.edge(e, r, "vch")
                r = e.tensor_mul(mksq_g[:, :], mksq_g[:, :], rn_g[:, :])
                p.edge(e, r, "vch")
                r = e.reciprocal(rmksq_g[:, :], mksq_g[:, :])
                p.inc(r, "ve", 1)
                for tc in range(NTC):
                    r = e.tensor_scalar_mul(mk_g[:, tc * 768:(tc + 1) * 768],
                                            mk_g[:, tc * 768:(tc + 1) * 768], rn_g[:, tc:tc + 1])
                    p.inc(r, "ve", 1)
                for kk in range(NTC * 6):
                    e.wait_ge(p.sem("pe"), pe0 + kk + 1)
                    tcc, fc = kk // 6, kk % 6
                    p.inc(e.tensor_copy(mkT_g[:, fc * T + tcc * 128:fc * T + (tcc + 1) * 128],
                                        pT[kk % 2][:, :]), "ev", 1)

            @blk.tensor
            def _(e):
                for tc in range(NTC):
                    e.wait_ge(p.sem("ve"), ve0 + 2 * NTC + 1 + tc + 1)
                    for fc in range(6):
                        kk = tc * 6 + fc
                        if kk >= 2:
                            e.wait_ge(p.sem("ev"), ev0 + kk - 1)
                        p.inc(e.transpose(pT[kk % 2][:, :],
                                          mk_g[:, tc * 768 + fc * 128:tc * 768 + (fc + 1) * 128],
                                          g["ident_bf"][:, :]), "pe", 1)
        stB.close()
        p.barrier()

        for hl in range(4):
            h = gi * 4 + hl
            stH = ExitStack()
            pmb = [p.ps(f"l0_pmb{gi}{hl}{i}", (128, 128), F32, stH) for i in range(2)]
            pnd = [p.ps(f"l0_pnd{gi}{hl}{i}", (128, 129), F32, stH) for i in range(2)]
            se0 = p.n("se")
            ve0 = p.n("ve")
            pe0 = p.n("pe")
            wtk = wmbq.k
            mtk = memq.k
            with nc.Block() as blk:
                @blk.gpsimd
                def _(e):
                    wmbq.issue(e, wmb_h[h % 2][:, :].rearrange("p (dc o) -> p dc o", dc=NDC), g["d_wmb_hsw"][h].rearrange("dc p o -> p dc o"))
                    memq.issue(e, mem_sb[h % 2][:, :].rearrange("q (fc c) -> q fc c", fc=6), g["d_mem_aug"][h].rearrange("(fc q) c -> q fc c", q=128))

                @blk.tensor
                def _(e):
                    wmbq.wait(e, wtk + 1)
                    memq.wait(e, mtk + 1)
                    for tc in range(NTC):
                        if tc >= 2:
                            e.wait_ge(p.sem("se"), se0 + tc - 1)
                        for dc in range(NDC):
                            e.matmul(pmb[tc % 2][:, :],
                                     xnT[:, dc * T + tc * 128:dc * T + (tc + 1) * 128],
                                     wmb_h[h % 2][:, dc * 128:(dc + 1) * 128],
                                     start=(dc == 0), stop=False)
                        mm = e.matmul(pmb[tc % 2][:, :], g["ones_bf"][0:1, 0:128],
                                      bmb_sb[0:1, h * 128:(h + 1) * 128], start=False, stop=True)
                        p.inc(mm, "pe", 1)
                        if tc >= 2:
                            e.wait_ge(p.sem("ve"), ve0 + 4 * (tc - 1))
                        for fc in range(6):
                            mm = e.matmul(pnd[tc % 2][:, :],
                                          mkT_g[:, fc * T + tc * 128:fc * T + (tc + 1) * 128],
                                          mem_sb[h % 2][:, fc * 129:(fc + 1) * 129],
                                          start=(fc == 0), stop=(fc == 5))
                        p.inc(mm, "pe", 1)

                @blk.scalar
                def _(e):
                    for tc in range(NTC):
                        e.wait_ge(p.sem("pe"), pe0 + 2 * tc + 1)
                        p.inc(e.activation(mb_h[:, tc * 128:(tc + 1) * 128], pmb[tc % 2][:, :],
                                           AF.Sigmoid), "se", 1)

                @blk.vector
                def _(e):
                    for tc in range(NTC):
                        e.wait_ge(p.sem("pe"), pe0 + 2 * tc + 2)
                        e.wait_ge(p.sem("se"), se0 + tc + 1)
                        r = e.tensor_scalar_add(den_g[:, hl * 16 + tc:hl * 16 + tc + 1],
                                                pnd[tc % 2][:, 128:129], EPS)
                        p.edge(e, r, "vch")
                        r = e.reciprocal(rden_c[:, :], den_g[:, hl * 16 + tc:hl * 16 + tc + 1])
                        p.edge(e, r, "vch")
                        r = e.tensor_scalar_mul(scr1[:, :], pnd[tc % 2][:, 0:128], rden_c[:, 0:1])
                        p.edge(e, r, "vch")
                        r = e.tensor_sub(scr2[:, :],
                                         v_all[:, tc * 512 + gi * 128:tc * 512 + (gi + 1) * 128],
                                         scr1[:, :])
                        p.edge(e, r, "vch")
                        r = e.tensor_mul(wmv_g[:, tc * 512 + hl * 128:tc * 512 + (hl + 1) * 128],
                                         scr2[:, :], mb_h[:, tc * 128:(tc + 1) * 128])
                        p.inc(r, "ve", 4)
                    r = e.tensor_mul(scr1[:, 0:16], den_g[:, hl * 16:hl * 16 + 16], rmksq_g[:, :])
                    p.edge(e, r, "vch")
                    r = e.tensor_scalar(scr2[:, 0:16], scr1[:, 0:16], -1.0, 1.0, ALU.mult, ALU.add)
                    p.edge(e, r, "vch")
                    r = e.tensor_scalar(coef_v[:, :, hl], scr2[:, 0:16], 0.0, 1.0, ALU.max, ALU.min)
                    p.inc(r, "ve", 1)
            stH.close()
            p.barrier()

        stD = ExitStack()
        pdel = [p.ps(f"l0_pdel{gi}_{fc}", (128, 512), F32, stD) for fc in range(6)]
        pdn = [p.ps(f"l0_pdn{gi}_{i}", (128, 384), F32, stD) for i in range(2)]
        pe0 = p.n("pe")
        ve0 = p.n("ve")
        atk0 = arq.k
        with nc.Block() as blk:
            @blk.tensor
            def _(e):
                for tc in range(NTC):
                    for fc in range(6):
                        mm = e.matmul(pdel[fc][:, :],
                                      mk_g[:, tc * 768 + fc * 128:tc * 768 + (fc + 1) * 128],
                                      wmv_g[:, tc * 512:(tc + 1) * 512],
                                      start=(tc == 0), stop=(tc == NTC - 1))
                    for i2 in range(2):
                        mm = e.matmul(pdn[i2][0:4, :], coef_g[:, tc * 4:(tc + 1) * 4],
                                      mk_g[:, tc * 768 + i2 * 384:tc * 768 + (i2 + 1) * 384],
                                      start=(tc == 0), stop=(tc == NTC - 1))
                p.inc(mm, "pe", 1)

            @blk.vector
            def _(e):
                e.wait_ge(p.sem("pe"), pe0 + 1)
                for fc in range(6):
                    gidx = gi * 6 + fc
                    if gidx >= 2:
                        for t4 in range(4):
                            arq.wait(e, art0 + 4 * (gidx - 2) + t4 + 1)
                    p.inc(e.tensor_copy(dsb2[fc % 2][:, :], pdel[fc][:, :]), "ve", 1)
                if gi >= 1:
                    dnq.wait(e, dnt0 + gi)
                r = e.tensor_copy(dnsb[:, 0:384], pdn[0][0:4, :])
                p.edge(e, r, "vch")
                p.inc(e.tensor_copy(dnsb[:, 384:768], pdn[1][0:4, :]), "ve", 1)

            @blk.sync
            def _(e):
                for fc in range(6):
                    e.wait_ge(p.sem("ve"), ve0 + fc + 1)
                    for hl in range(4):
                        hh = gi * 4 + hl
                        arq.issue(e, ar_in[hh * 768 + fc * 128:hh * 768 + (fc + 1) * 128, :],
                                  dsb2[fc % 2][:, hl * 128:(hl + 1) * 128])
                e.wait_ge(p.sem("ve"), ve0 + 7)
                dnq.issue(e, ar_in[12288 + gi * 24:12288 + (gi + 1) * 24, :]
                          .rearrange("(hl r) c -> hl (r c)", r=6), dnsb[:, :])
        stD.close()
        p.barrier()
    with nc.Block() as blk:
        @blk.sync
        def _(e):
            arq.wait_all(e)
            dnq.wait_all(e)
    p.barrier()
    st.close()



def emit_ar(p, ar_in, ar_out):
    arq = p.dq("s4", 8)
    dnq = p.dq("s2", 4)
    with p.nc.Block() as blk:
        @blk.gpsimd
        def _(e):
            arq.wait_all(e)
            dnq.wait_all(e)
            cc = e.collective_compute(
                "AllReduce", ALU.add, replica_groups=[list(range(NCORES))],
                ins=[ar_in.opt()], outs=[ar_out.opt()])
            p.inc(cc, "cc", 1)
    p.barrier()


def emit_newmem(p, g, ar_out, nm_aug_sb):
    nc = p.nc
    st = ExitStack()
    dn_in = p.sb("nm_dnin", (128, 128), F32, st)
    dnT = p.sb("nm_dnT", (128, 128), F32, st)
    pdT = p.ps("nm_pdT", (128, 128), F32, st)
    mem_bf = p.sb("nm_mem", (128, 96 * 129), BF16, st)
    dar = p.sb("nm_dar", (128, 96 * 128), F32, st)
    q = p.dq("s4", 8)
    qg = p.dq("g2", 4)
    qt0 = q.k
    qg0 = qg.k
    pe0 = p.n("pe")
    ve0 = p.n("ve")
    vz0 = p.n("vz")
    with nc.Block() as blk:
        @blk.gpsimd
        def _(e):
            for i in range(4):
                qg.issue(e, mem_bf[:, i * 24 * 129:(i + 1) * 24 * 129]
                         .rearrange("q (j c) -> q j c", j=24),
                         g["d_mem_aug"].rearrange("h (fc q) c -> q (h fc) c", q=128)
                         [:, i * 24:(i + 1) * 24, :])

        @blk.sync
        def _(e):
            e.wait_ge(p.sem("cc"), p.n("cc"))
            e.wait_ge(p.sem("vz"), vz0 + 1)
            q.issue(e, dn_in[0:96, :], ar_out[12288:12384, :])
            for i in range(4):
                q.issue(e, dar[:, i * 24 * 128:(i + 1) * 24 * 128]
                        .rearrange("q (j d) -> q j d", j=24),
                        ar_out[0:12288, :].rearrange("(j q) d -> q j d", q=128)
                        [:, i * 24:(i + 1) * 24, :])

        @blk.tensor
        def _(e):
            q.wait(e, qt0 + 1)
            p.inc(e.transpose(pdT[:, :], dn_in[:, :], g["ident_f32"][:, :]), "pe", 1)

        @blk.vector
        def _(e):
            p.inc(e.memset(dn_in[:, :], 0.0), "vz", 1)
            e.wait_ge(p.sem("pe"), pe0 + 1)
            r = e.tensor_copy(dnT[:, :], pdT[:, :])
            p.edge(e, r, "vch")
            for i in range(4):
                qg.wait(e, qg0 + i + 1)
                q.wait(e, qt0 + i + 2)
            for j in range(96):
                r = e.tensor_add(nm_aug_sb[:, j * 129:j * 129 + 128],
                                 mem_bf[:, j * 129:j * 129 + 128], dar[:, j * 128:(j + 1) * 128])
                p.edge(e, r, "vch")
                r = e.tensor_add(nm_aug_sb[:, j * 129 + 128:j * 129 + 129],
                                 mem_bf[:, j * 129 + 128:j * 129 + 129], dnT[:, j:j + 1])
                p.edge(e, r, "vch")
            p.inc(e.drain(), "ve", 1)
    st.close()
    p.barrier()



def emit_l0p2(p, xnT, g, nm_aug, x2d, g0_dram, x0_dram, x1_dram):
    nc = p.nc
    st = ExitStack()
    wq_sb = [p.sb(f"p2_wq{i}", (128, NDC * 128), BF16, st) for i in range(2)]
    wg_sb = p.sb("p2_wg", (128, NDC * 512), BF16, st)
    gate_hb = p.sb("p2_gate", (128, NTC * 512), BF16, st)
    g0_hb = p.sb("p2_g0", (128, NTC * 512), BF16, st)
    x0t = [p.sb(f"p2_x0t{i}", (128, 128), F32, st) for i in range(4)]
    bq_cols = p.sb("p2_bqc", (128, 16), F32, st)
    nbq_cols = p.sb("p2_nbqc", (128, 16), F32, st)
    bg_row = p.sb("p2_bgr", (1, 2048), BF16, st)
    x2a = p.sb("p2_x2a", (128, T), BF16, st)
    x2b = p.sb("p2_x2b", (128, T), BF16, st)
    x2r = [p.sb("p2_x2r0", (128, T), BF16, st)] * 2
    dT = p.sb("p2_dT", (128, 6 * T), BF16, st)
    dsq = [p.sb("p2_dsq0", (128, T), BF16, st)] * 2
    rnrow = p.sb("p2_rnrow", (1, T), F32, st)
    rn16 = p.sb("p2_rn16", (16, 128), F32, st)
    rnT = p.sb("p2_rnT", (128, 16), F32, st)
    dfin = p.sb("p2_dfin", (128, 1), F32, st)
    rden = p.sb("p2_rden", (128, 1), F32, st)
    rr = p.sb("p2_rr", (128, 1), F32, st)
    memh = p.sb("p2_memh", (128, 128), F32, st)
    sc1 = p.sb("p2_sc1", (128, 128), F32, st)
    sc2 = [p.sb(f"p2_sc2{i}", (128, 128), F32, st) for i in range(2)]
    sc3 = p.sb("p2_sc3", (128, 128), F32, st)
    x1t = [p.sb(f"p2_x1t{i}", (128, 128), F32, st) for i in range(2)]

    g["d_rnscr"] = p.dram("d_rnscr", (1, T), F32).ap()
    wqq = p.dq("g1", 8)
    wgq = p.dq("g1", 8)
    hbq = p.dq("s5", 8)
    xdq = p.dq("s1", 4)
    xrq = p.dq("s1", 4)
    x1q = p.dq("s3", 4)
    x0q = p.dq("s4", 8)
    qvq = p.dq("s2", 4)

    cst = p.dq("s2", 4)
    cbg = p.dq("g2", 4)
    ct0 = cst.k
    with nc.Block() as blk:
        @blk.sync
        def _(e):
            cst.issue(e, bq_cols[:, :], g["d_bq_cols"][:, :])

        @blk.gpsimd
        def _(e):
            cbg.issue(e, bg_row[:, :], g["d_bg_row"][:, :])
            cbg.wait_all(e)

        @blk.vector
        def _(e):
            cst.wait(e, ct0 + 1)
            p.inc(e.tensor_scalar_mul(nbq_cols[:, :], bq_cols[:, :], -1.0), "ve", 1)
    p.barrier()

    stP = ExitStack()
    pq = [p.ps(f"p2_pq{i}", (128, 512), F32, stP) for i in range(4)]
    pnd = [p.ps(f"p2_pnd{i}", (128, 129), F32, stP) for i in range(2)]
    pxT = [p.ps(f"p2_pxT{i}", (128, 128), BF16, stP) for i in range(2)]
    xn_tok = [p.sb(f"p2_xnt{i}", (128, 128), BF16, st) for i in range(2)]

    PEH = 61
    SEH = 13
    VEH = 17
    VPH = 16
    GEH = 16
    VQH = 6

    for hb in range(4):
        se0 = p.n("se")
        pe0 = p.n("pe")
        wgt0 = wgq.k
        with nc.Block() as blk:
            @blk.gpsimd
            def _(e):
                wgq.issue(e, wg_sb[:, :].rearrange("p (dc o) -> p dc o", dc=NDC),
                          g["d_wg_gsw"][hb].rearrange("dc p o -> p dc o"))

            @blk.sync
            def _(e):
                for tc in range(NTC):
                    hbq.issue(e, g0_hb[:, tc * 512:(tc + 1) * 512],
                              g0_dram[tc * 128:(tc + 1) * 128, hb * 512:(hb + 1) * 512])

            @blk.tensor
            def _(e):
                wgq.wait(e, wgt0 + 1)
                for tc in range(NTC):
                    if tc >= 2:
                        e.wait_ge(p.sem("se"), se0 + tc - 1)
                    for dc in range(NDC):
                        e.matmul(pq[tc % 2][:, :],
                                 xnT[:, dc * T + tc * 128:dc * T + (tc + 1) * 128],
                                 wg_sb[:, dc * 512:(dc + 1) * 512], start=(dc == 0), stop=False)
                    mm = e.matmul(pq[tc % 2][:, :], g["ones_bf"][0:1, 0:128],
                                  bg_row[0:1, hb * 512:(hb + 1) * 512], start=False, stop=True)
                    p.inc(mm, "pe", 1)

            @blk.scalar
            def _(e):
                for tc in range(NTC):
                    e.wait_ge(p.sem("pe"), pe0 + tc + 1)
                    p.inc(e.activation(gate_hb[:, tc * 512:(tc + 1) * 512], pq[tc % 2][:, :],
                                       AF.Sigmoid), "se", 1)
        p.barrier()

        seB = p.n("se")
        peB = p.n("pe")
        veB = p.n("ve")
        vpB = p.n("vp")
        geB = p.n("ge")
        vqB = p.n("vq")
        wqt0 = wqq.k
        xdt0 = xdq.k
        xrt0 = xrq.k
        qvt0 = qvq.k
        x1t0 = x1q.k
        x0t0 = x0q.k
        with nc.Block() as blk:
            @blk.gpsimd
            def _(e):
                for hl in range(2):
                    wqq.issue(e, wq_sb[hl % 2][:, :].rearrange("p (dc o) -> p dc o", dc=NDC),
                              g["d_wq_hsw"][hb * 4 + hl].rearrange("dc p o -> p dc o"))
                hbq.wait_all(e)
                for hl in range(4):
                    h = hb * 4 + hl
                    if hl + 2 < 4:
                        e.wait_ge(p.sem("pe"), peB + hl * PEH + 4)
                        wqq.issue(e, wq_sb[hl % 2][:, :].rearrange("p (dc o) -> p dc o", dc=NDC),
                                  g["d_wq_hsw"][hb * 4 + hl + 2].rearrange("dc p o -> p dc o"))
                    for tc in range(NTC):
                        gtc = hl * 16 + tc
                        e.wait_ge(p.sem("ve"), veB + hl * VEH + 1 + tc + 1)
                        if gtc >= 1:
                            e.wait_ge(p.sem("ge"), geB + gtc)
                        r = e.tensor_mul(sc3[:, :], sc2[tc % 2][:, :],
                                         g0_hb[:, tc * 512 + hl * 128:tc * 512 + (hl + 1) * 128])
                        p.edge(e, r, "gch")
                        xt1 = x1t0 + gtc - 1
                        if xt1 >= 1:
                            x1q.wait(e, xt1)
                        x0q.wait(e, x0t0 + gtc + 1)
                        r = e.tensor_add(x1t[tc % 2][:, :], sc3[:, :], x0t[tc % 4][:, :])
                        p.inc(r, "ge", 1)

            @blk.tensor
            def _(e):
                for hl in range(4):
                    h = hb * 4 + hl
                    wqq.wait(e, wqt0 + hl + 1)
                    for sp in range(4):
                        if hl >= 1:
                            e.wait_ge(p.sem("se"), seB + (hl - 1) * SEH + 8 + sp + 1)
                        for dc in range(NDC):
                            mm = e.matmul(pq[sp][:, :], wq_sb[hl % 2][:, dc * 128:(dc + 1) * 128],
                                          xnT[:, dc * T + sp * 512:dc * T + (sp + 1) * 512],
                                          start=(dc == 0), stop=(dc == NDC - 1))
                        p.inc(mm, "pe", 1)
                    for fc in range(6):
                        e.wait_ge(p.sem("vq"), vqB + hl * VQH + fc + 1)
                        for sp in range(4):
                            mm = e.matmul(pq[sp][0:1, :], g["ones_col"][:, 0:1],
                                          dsq[fc % 2][:, sp * 512:(sp + 1) * 512],
                                          start=(fc == 0), stop=(fc == 5))
                            p.inc(mm, "pe", 1)
                    qvq.wait(e, qvt0 + 2 * hl + 2)
                    p.inc(e.transpose(pnd[0][:, 0:16], rn16[:, :], g["ident_f32"][0:16, 0:16]), "pe", 1)
                    e.wait_ge(p.sem("se"), seB + hl * SEH + 13)
                    for tc in range(NTC):
                        gtc = hl * 16 + tc
                        if gtc >= 2:
                            e.wait_ge(p.sem("vp"), vpB + gtc - 1)
                        for fc in range(6):
                            mm = e.matmul(pnd[tc % 2][:, :],
                                          dT[:, fc * T + tc * 128:fc * T + (tc + 1) * 128],
                                          nm_aug[:, (h * 6 + fc) * 129:(h * 6 + fc + 1) * 129],
                                          start=(fc == 0), stop=(fc == 5))
                        p.inc(mm, "pe", 1)
                        p.inc(e.transpose(pxT[tc % 2][:, :],
                                          xnT[:, h * T + tc * 128:h * T + (tc + 1) * 128],
                                          g["ident_bf"][:, :]), "pe", 1)

            @blk.scalar
            def _(e):
                for hl in range(4):
                    h = hb * 4 + hl
                    for sp in range(4):
                        e.wait_ge(p.sem("pe"), peB + hl * PEH + sp + 1)
                        e.wait_ge(p.sem("vq"), vqB + hl * VQH)
                        p.inc(e.activation(x2a[:, sp * 512:(sp + 1) * 512], pq[sp][:, :],
                                           AF.Relu, bias=bq_cols[:, h:h + 1]), "se", 1)
                        p.inc(e.activation(x2b[:, sp * 512:(sp + 1) * 512], pq[sp][:, :],
                                           AF.Relu, bias=nbq_cols[:, h:h + 1],
                                           scale=g["neg1_col"][:, 0:1]), "se", 1)
                    if hl >= 1:
                        qvq.wait(e, qvt0 + 2 * (hl - 1) + 1)
                        e.wait_ge(p.sem("ve"), veB + (hl - 1) * VEH + 1)
                    for sp in range(4):
                        e.wait_ge(p.sem("pe"), peB + hl * PEH + 4 + 20 + sp + 1)
                        p.inc(e.activation(rnrow[0:1, sp * 512:(sp + 1) * 512], pq[sp][0:1, :],
                                           AF.Sqrt), "se", 1)
                    e.wait_ge(p.sem("pe"), peB + hl * PEH + 4 + 24 + 1)
                    if hl >= 1:
                        e.wait_ge(p.sem("ve"), veB + hl * VEH)
                    p.inc(e.activation(rnT[:, :], pnd[0][:, 0:16], AF.Copy), "se", 1)

            @blk.sync
            def _(e):
                for hl in range(4):
                    h = hb * 4 + hl
                    e.wait_ge(p.sem("se"), seB + hl * SEH + 8)
                    e.wait_ge(p.sem("vq"), vqB + hl * VQH)
                    xdq.issue(e, x2d[0:128, :], x2a[:, :])
                    xdq.issue(e, x2d[128:256, :], x2b[:, :])
                    xdq.issue(e, x2d[256:384, :], x2a[:, :])
                    xdq.issue(e, x2d[384:512, :], x2b[:, :])
                    for t4 in range(4):
                        xdq.wait(e, xdt0 + 10 * hl + t4 + 1)
                    for fc in range(6):
                        j = fc // 2 + 1
                        sr = 256 + (fc % 2) * 128 - j
                        if fc >= 1:
                            e.wait_ge(p.sem("vq"), vqB + hl * VQH + fc)
                        xrq.issue(e, x2r[fc % 2][:, :], x2d[sr:sr + 128, :])
                    e.wait_ge(p.sem("ve"), veB + hl * VEH + 1)
                    t1_ = qvq.issue(e, g["d_rnscr"][:, :], rnrow[0:1, :])
                    qvq.wait(e, t1_)
                    if hl >= 1:
                        e.wait_ge(p.sem("pe"), peB + (hl - 1) * PEH + 29)
                    qvq.issue(e, rn16[:, :], g["d_rnscr"].rearrange("q (tc c) -> (q tc) c", tc=16))
                    for tc in range(4):
                        gtc = hl * 16 + tc
                        x0q.issue(e, x0t[tc][:, :],
                                  x0_dram[tc * 128:(tc + 1) * 128, h * 128:(h + 1) * 128])
                    for tc in range(NTC):
                        gtc = hl * 16 + tc
                        e.wait_ge(p.sem("ge"), geB + gtc + 1)
                        x1q.issue(e, x1_dram[tc * 128:(tc + 1) * 128, h * 128:(h + 1) * 128],
                                  x1t[tc % 2][:, :])
                        if tc + 4 < NTC:
                            x0q.issue(e, x0t[(tc + 4) % 4][:, :],
                                      x0_dram[(tc + 4) * 128:(tc + 5) * 128, h * 128:(h + 1) * 128])

            @blk.vector
            def _(e):
                for hl in range(4):
                    h = hb * 4 + hl
                    xr0 = xrt0 + 10 * hl + 4
                    for fc in range(6):
                        xrq.wait(e, xr0 + fc + 1)
                        src = x2a if fc % 2 == 0 else x2b
                        if fc == 0 and hl >= 1:
                            e.wait_ge(p.sem("pe"), peB + hl * PEH)
                        r = e.tensor_mul(dT[:, fc * T:(fc + 1) * T], src[:, :],
                                         x2r[fc % 2][:, :])
                        p.edge(e, r, "vch")
                        if fc >= 1:
                            e.wait_ge(p.sem("pe"), peB + hl * PEH + 4 + 4 * fc)
                        elif hl >= 1:
                            e.wait_ge(p.sem("pe"), peB + (hl - 1) * PEH + 28)
                        p.inc(e.tensor_mul(dsq[fc % 2][:, :], dT[:, fc * T:(fc + 1) * T],
                                           dT[:, fc * T:(fc + 1) * T]), "vq", 1)
                    e.wait_ge(p.sem("se"), seB + hl * SEH + 12)
                    r = e.tensor_scalar_max(rnrow[:, :], rnrow[:, :], 1e-12)
                    p.edge(e, r, "vch")
                    p.inc(e.reciprocal(rnrow[:, :], rnrow[:, :]), "ve", 1)
                    for tc in range(NTC):
                        gtc = hl * 16 + tc
                        e.wait_ge(p.sem("pe"), peB + hl * PEH + 29 + 2 * tc + 2)
                        if gtc >= 2:
                            e.wait_ge(p.sem("ge"), geB + gtc - 1)
                        r = e.tensor_copy(xn_tok[tc % 2][:, :], pxT[tc % 2][:, :])
                        p.edge(e, r, "vch")
                        r = e.tensor_scalar(dfin[:, :], pnd[tc % 2][:, 128:129],
                                            rnT[:, tc:tc + 1], EPS, ALU.mult, ALU.add)
                        p.edge(e, r, "vch")
                        r = e.reciprocal(rden[:, :], dfin[:, :])
                        p.edge(e, r, "vch")
                        r = e.tensor_mul(rr[:, :], rnT[:, tc:tc + 1], rden[:, :])
                        p.edge(e, r, "vch")
                        r = e.tensor_scalar_mul(memh[:, :], pnd[tc % 2][:, 0:128], rr[:, 0:1])
                        p.edge(e, r, "vp")
                        r = e.tensor_mul(sc1[:, :], gate_hb[:, tc * 512 + hl * 128:tc * 512 + (hl + 1) * 128],
                                         memh[:, :])
                        p.edge(e, r, "vch")
                        r = e.tensor_add(sc2[tc % 2][:, :], sc1[:, :], xn_tok[tc % 2][:, :])
                        p.inc(r, "ve", 1)
        p.barrier()
    stP.close()
    with nc.Block() as blk:
        @blk.sync
        def _(e):
            x1q.wait_all(e)
    p.barrier()
    st.close()



def emit_l1kv(p, xnT, g, ag_in, ag_out):
    nc = p.nc
    st = ExitStack()
    wk1 = [p.sb(f"kv_wk{i}", (128, NDC * 128), BF16, st) for i in range(2)]
    wv1 = p.sb("kv_wv", (128, NDC * 2048), BF16, st)
    bk1r = p.sb("kv_bk1r", (1, 2048), BF16, st)
    bv1 = p.sb("kv_bv1", (1, 2048), BF16, st)
    kt_sb = [p.sb(f"kv_kt{i}", (128, T), BF16, st) for i in range(2)]
    vt_sb = [p.sb(f"kv_vt{i}", (128, T), BF16, st) for i in range(2)]
    wq1 = p.dq("g1", 8)
    agq = p.dq("s5", 8)

    stK = ExitStack()
    pk = [p.ps(f"kv_pk{i}", (128, 512), F32, stK) for i in range(4)]
    se0 = p.n("se")
    pe0 = p.n("pe")
    wt0 = wq1.k
    akt0 = agq.k
    with nc.Block() as blk:
        @blk.gpsimd
        def _(e):
            kvb = p.dq("g2", 4)
            kvb.issue(e, bv1[:, :], g["d_bv1_row"][:, :])
            kvb.issue(e, bk1r[:, :], g["d_bk1_rows"][:, :])
            for oc in range(16):
                if oc >= 2:
                    e.wait_ge(p.sem("pe"), pe0 + 4 * (oc - 1))
                wq1.issue(e, wk1[oc % 2][:, :].rearrange("p (dc o) -> p dc o", dc=NDC),
                          g["d_wk1_hsw"][oc].rearrange("dc p o -> p dc o"))

        @blk.tensor
        def _(e):
            p.dq("g2", 4).wait_all(e)
            for oc in range(16):
                wq1.wait(e, wt0 + oc + 1)
                for sp in range(4):
                    if oc >= 1:
                        e.wait_ge(p.sem("se"), se0 + 4 * (oc - 1) + sp + 1)
                    for dc in range(NDC):
                        e.matmul(pk[sp][:, :], wk1[oc % 2][:, dc * 128:(dc + 1) * 128],
                                 xnT[:, dc * T + sp * 512:dc * T + (sp + 1) * 512],
                                 start=(dc == 0), stop=False)
                    mm = e.matmul(pk[sp][:, :], bk1r[0:1, oc * 128:(oc + 1) * 128], g["ones_bf"][0:1, :],
                                  start=False, stop=True)
                    p.inc(mm, "pe", 1)

        @blk.scalar
        def _(e):
            for oc in range(16):
                for sp in range(4):
                    e.wait_ge(p.sem("pe"), pe0 + 4 * oc + sp + 1)
                    if oc >= 2 and sp == 0:
                        agq.wait(e, akt0 + oc - 1)
                    p.inc(e.activation(kt_sb[oc % 2][:, sp * 512:(sp + 1) * 512], pk[sp][:, :],
                                       AF.Copy), "se", 1)

        @blk.sync
        def _(e):
            for oc in range(16):
                e.wait_ge(p.sem("se"), se0 + 4 * (oc + 1))
                agq.issue(e, ag_in[oc * 128:(oc + 1) * 128, :], kt_sb[oc % 2][:, :])
    stK.close()
    p.barrier()

    stV = ExitStack()
    pv = [p.ps(f"kv_pv{i}_{j}", (128, 512), F32, stV) for i in range(2) for j in range(4)]
    se0 = p.n("se")
    pe0 = p.n("pe")
    wt0 = wq1.k
    agt0 = agq.k
    with nc.Block() as blk:
        @blk.gpsimd
        def _(e):
            wq1.issue(e, wv1[:, :].rearrange("p (dc o) -> p dc o", dc=NDC),
                      g["d_wv1_sw"].rearrange("dc p o -> p dc o"))

        @blk.tensor
        def _(e):
            wq1.wait(e, wt0 + 1)
            for tc in range(NTC):
                if tc >= 2:
                    e.wait_ge(p.sem("se"), se0 + 4 * (tc - 1))
                for ob in range(4):
                    pvx = pv[(tc % 2) * 4 + ob]
                    for dc in range(NDC):
                        e.matmul(pvx[:, :], xnT[:, dc * T + tc * 128:dc * T + (tc + 1) * 128],
                                 wv1[:, dc * 2048 + ob * 512:dc * 2048 + (ob + 1) * 512],
                                 start=(dc == 0), stop=False)
                    mm = e.matmul(pvx[:, :], g["ones_bf"][0:1, 0:128],
                                  bv1[0:1, ob * 512:(ob + 1) * 512], start=False, stop=True)
                    p.inc(mm, "pe", 1)

        @blk.scalar
        def _(e):
            for tc in range(NTC):
                if tc >= 2:
                    agq.wait(e, agt0 + tc - 1)
                for ob in range(4):
                    e.wait_ge(p.sem("pe"), pe0 + 4 * tc + ob + 1)
                    p.inc(e.activation(vt_sb[tc % 2][:, ob * 512:(ob + 1) * 512],
                                       pv[(tc % 2) * 4 + ob][:, :], AF.Copy), "se", 1)

        @blk.sync
        def _(e):
            for tc in range(NTC):
                e.wait_ge(p.sem("se"), se0 + 4 * (tc + 1))
                agq.issue(e, ag_in[2048 + tc * 128:2048 + (tc + 1) * 128, :], vt_sb[tc % 2][:, :])
    stV.close()
    p.barrier()
    st.close()

    with nc.Block() as blk:
        @blk.gpsimd
        def _(e):
            agq.wait_all(e)
            cc = e.collective_compute(
                "AllGather", ALU.bypass,
                replica_groups=[[0, 1], [2, 3], [4, 5], [6, 7]],
                ins=[ag_in.opt()], outs=[ag_out.opt()])
            p.inc(cc, "cc", 1)
    p.barrier()



def emit_ffn1(p, xnT, g, g1_dram):
    nc = p.nc
    NI = 44
    st = ExitStack()
    fuse = p.sb("f1_fuse", (128, NI * 1024), BF16, st)
    wgs = [p.sb(f"f1_wg{i}", (128, DM), BF16, st) for i in range(3)]
    wus = [p.sb(f"f1_wu{i}", (128, DM), BF16, st) for i in range(3)]
    wds = [p.sb(f"f1_wd{i}", (128, 512), BF16, st) for i in range(4)]
    sg = p.sb("f1_sg", (128, NI), F32, st)
    su = p.sb("f1_su", (128, NI), F32, st)
    sdb = p.sb("f1_sdb", (128, DM), F32, st)
    gsc = [p.sb(f"f1_gsc{i}", (128, 1024), BF16, st) for i in range(2)]
    usc = [p.sb(f"f1_usc{i}", (128, 1024), BF16, st) for i in range(2)]
    ysc = [p.sb(f"f1_ysc{i}", (128, 512), F32, st) for i in range(2)]
    g1t = [p.sb(f"f1_g1t{i}", (128, 512), BF16, st) for i in range(2)]

    scq = p.dq("s2", 4)
    with nc.Block() as blk:
        @blk.sync
        def _(e):
            scq.issue(e, sg[:, :], g["d_sg1_sw"][:, :])
            scq.issue(e, su[:, :], g["d_su1_sw"][:, :])
            scq.issue(e, sdb[:, :], g["d_sd1_bc"][:, :])
            scq.wait_all(e)
    p.barrier()

    wq = p.dq("g1", 8)
    wdq = p.dq("g1", 8)
    g1q = p.dq("s3", 4)
    for half in range(2):
        t0 = half * 1024
        stq = ExitStack()
        pg = [[p.ps(f"f1_pg{half}_{i}_{s2}", (128, 512), F32, stq) for s2 in range(2)] for i in range(2)]
        pu = [[p.ps(f"f1_pu{half}_{i}_{s2}", (128, 512), F32, stq) for s2 in range(2)] for i in range(2)]
        se0 = p.n("se")
        ve0 = p.n("ve")
        vw0 = p.n("vw")
        pe0 = p.n("pe")
        wt0 = wq.k
        with nc.Block() as blk:
            @blk.gpsimd
            def _(e):
                for ic in range(NI):
                    if ic >= 3:
                        e.wait_ge(p.sem("pe"), pe0 + 4 * (ic - 2))
                    wq.issue(e, wgs[ic % 3][:, :], g["d_wg1_sw"][ic])
                    wq.issue(e, wus[ic % 3][:, :], g["d_wu1_sw"][ic])

            @blk.vector
            def _(e):
                for ic in range(NI):
                    wq.wait(e, wt0 + 2 * ic + 1)
                    wq.wait(e, wt0 + 2 * ic + 2)
                    if ic >= 2:
                        e.wait_ge(p.sem("pe"), pe0 + 4 * (ic - 1))
                    r = e.tensor_scalar_add(wgs[ic % 3][:, :], wgs[ic % 3][:, :], -128.0)
                    p.inc(r, "vw", 1)
                    r = e.tensor_scalar_add(wus[ic % 3][:, :], wus[ic % 3][:, :], -128.0)
                    p.inc(r, "vw", 1)
                    if ic >= 1:
                        e.wait_ge(p.sem("se"), se0 + 4 * ic)
                        p.inc(e.tensor_mul(fuse[:, (ic - 1) * 1024:ic * 1024],
                                           gsc[(ic - 1) % 2][:, :], usc[(ic - 1) % 2][:, :]), "ve", 1)
                e.wait_ge(p.sem("se"), se0 + 4 * NI)
                p.inc(e.tensor_mul(fuse[:, (NI - 1) * 1024:NI * 1024],
                                   gsc[(NI - 1) % 2][:, :], usc[(NI - 1) % 2][:, :]), "ve", 1)

            @blk.tensor
            def _(e):
                for ic in range(NI):
                    e.wait_ge(p.sem("vw"), vw0 + 2 * (ic + 1))
                    if ic >= 2:
                        e.wait_ge(p.sem("se"), se0 + 4 * (ic - 1))
                    for s2 in range(2):
                        for dc in range(NDC):
                            mm = e.matmul(pg[ic % 2][s2][:, :], wgs[ic % 3][:, dc * 128:(dc + 1) * 128],
                                          xnT[:, dc * T + t0 + s2 * 512:dc * T + t0 + (s2 + 1) * 512],
                                          start=(dc == 0), stop=(dc == NDC - 1))
                        p.inc(mm, "pe", 1)
                    for s2 in range(2):
                        for dc in range(NDC):
                            mm = e.matmul(pu[ic % 2][s2][:, :], wus[ic % 3][:, dc * 128:(dc + 1) * 128],
                                          xnT[:, dc * T + t0 + s2 * 512:dc * T + t0 + (s2 + 1) * 512],
                                          start=(dc == 0), stop=(dc == NDC - 1))
                        p.inc(mm, "pe", 1)

            @blk.scalar
            def _(e):
                for ic in range(NI):
                    if ic >= 2:
                        e.wait_ge(p.sem("ve"), ve0 + ic)
                    for s2 in range(2):
                        e.wait_ge(p.sem("pe"), pe0 + 4 * ic + s2 + 1)
                        p.inc(e.activation(gsc[ic % 2][:, s2 * 512:(s2 + 1) * 512],
                                           pg[ic % 2][s2][:, :], AF.Silu, scale=sg[:, ic:ic + 1]), "se", 1)
                    for s2 in range(2):
                        e.wait_ge(p.sem("pe"), pe0 + 4 * ic + s2 + 3)
                        p.inc(e.activation(usc[ic % 2][:, s2 * 512:(s2 + 1) * 512],
                                           pu[ic % 2][s2][:, :], AF.Copy, scale=su[:, ic:ic + 1]), "se", 1)
        stq.close()
        p.barrier()

        for oq in range(4):
            std = ExitStack()
            pd = [p.ps(f"f1_pd{half}_{oq}_{t8}", (128, 512), F32, std) for t8 in range(8)]
            pe0 = p.n("pe")
            ve0 = p.n("ve")
            vw0 = p.n("vw")
            se0 = p.n("se")
            dt0 = wdq.k
            gt0 = g1q.k
            with nc.Block() as blk:
                @blk.gpsimd
                def _(e):
                    for ic in range(NI):
                        if ic >= 4:
                            e.wait_ge(p.sem("pe"), pe0 + (ic - 3))
                        wdq.issue(e, wds[ic % 4][:, :], g["d_wd1_sw"][oq, ic])

                @blk.vector
                def _(e):
                    for ic in range(NI):
                        wdq.wait(e, dt0 + ic + 1)
                        if ic >= 4:
                            e.wait_ge(p.sem("pe"), pe0 + (ic - 3))
                        p.inc(e.tensor_scalar_add(wds[ic % 4][:, :], wds[ic % 4][:, :], -128.0), "vw", 1)
                    e.wait_ge(p.sem("pe"), pe0 + NI)
                    for t8 in range(8):
                        if t8 >= 2:
                            e.wait_ge(p.sem("se"), se0 + t8 - 1)
                        p.inc(e.tensor_mul(ysc[t8 % 2][:, :], pd[t8][:, :],
                                           sdb[:, oq * 512:(oq + 1) * 512]), "ve", 1)

                @blk.tensor
                def _(e):
                    for ic in range(NI):
                        e.wait_ge(p.sem("vw"), vw0 + ic + 1)
                        for t8 in range(8):
                            mm = e.matmul(pd[t8][:, :],
                                          fuse[:, ic * 1024 + t8 * 128:ic * 1024 + (t8 + 1) * 128],
                                          wds[ic % 4][:, :],
                                          start=(ic == 0), stop=(ic == NI - 1))
                        p.inc(mm, "pe", 1)

                @blk.scalar
                def _(e):
                    for t8 in range(8):
                        e.wait_ge(p.sem("ve"), ve0 + t8 + 1)
                        gl = gt0 + t8 - 1
                        if gl >= 1:
                            g1q.wait(e, gl)
                        p.inc(e.activation(g1t[t8 % 2][:, :], ysc[t8 % 2][:, :], AF.Silu), "se", 1)

                @blk.sync
                def _(e):
                    for t8 in range(8):
                        e.wait_ge(p.sem("se"), se0 + t8 + 1)
                        g1q.issue(e, g1_dram[t0 + t8 * 128:t0 + (t8 + 1) * 128,
                                             oq * 512:(oq + 1) * 512], g1t[t8 % 2][:, :])
            std.close()
            p.barrier()
    with nc.Block() as blk:
        @blk.sync
        def _(e):
            g1q.wait_all(e)
    p.barrier()
    st.close()



def emit_l1q(p, xnT, g, attnout):
    nc = p.nc
    st = ExitStack()
    wqh = [p.sb(f"at_wq{i}", (128, NDC * 128), BF16, st) for i in range(2)]
    bq1r = p.sb("at_bq1r", (1, 2048), BF16, st)
    wq2 = p.dq("g1", 8)
    cq = p.dq("g2", 4)
    stQ = ExitStack()
    pq = [p.ps(f"at_pq{i}", (128, 512), F32, stQ) for i in range(4)]
    se0 = p.n("se")
    pe0 = p.n("pe")
    wt0 = wq2.k
    ct0 = cq.k
    with nc.Block() as blk:
        @blk.gpsimd
        def _(e):
            cq.issue(e, bq1r[:, :], g["d_bq1_rows"][:, :])
            for h in range(16):
                if h >= 2:
                    e.wait_ge(p.sem("pe"), pe0 + 4 * (h - 1))
                wq2.issue(e, wqh[h % 2][:, :].rearrange("p (dc o) -> p dc o", dc=NDC),
                          g["d_wq1_hsw"][h].rearrange("dc p o -> p dc o"))

        @blk.tensor
        def _(e):
            cq.wait(e, ct0 + 1)
            for h in range(16):
                wq2.wait(e, wt0 + h + 1)
                for sp in range(4):
                    k4 = 4 * h + sp
                    if k4 >= 4:
                        e.wait_ge(p.sem("se"), se0 + k4 - 3)
                    for dc in range(NDC):
                        e.matmul(pq[sp][:, :], wqh[h % 2][:, dc * 128:(dc + 1) * 128],
                                 xnT[:, dc * T + sp * 512:dc * T + (sp + 1) * 512],
                                 start=(dc == 0), stop=False)
                    mm = e.matmul(pq[sp][:, :], bq1r[0:1, h * 128:(h + 1) * 128],
                                  g["ones_bf"][0:1, :], start=False, stop=True)
                    p.inc(mm, "pe", 1)

        @blk.scalar
        def _(e):
            for h in range(16):
                for sp in range(4):
                    k4 = 4 * h + sp
                    e.wait_ge(p.sem("pe"), pe0 + k4 + 1)
                    p.inc(e.activation(attnout[:, h * 2048 + sp * 512:h * 2048 + (sp + 1) * 512],
                                       pq[sp][:, :], AF.Copy), "se", 1)
    stQ.close()
    p.barrier()
    st.close()


def emit_l1attn(p, g, ag_out, attnout):
    nc = p.nc
    SQ = 4096
    st = ExitStack()
    kh = [p.sb(f"at_k{i}", (128, SQ), BF16, st) for i in range(2)]
    vh = [p.sb(f"at_v{i}", (128, SQ), BF16, st) for i in range(2)]
    probs = [p.sb("at_probs0", (128, 32 * 512), BF16, st)] * 2
    rdrow = p.sb("at_rdrow", (1, 512), F32, st)
    rdb = p.sb("at_rdb", (128, 512), F32, st)
    psc = [p.ps(f"at_psc{i}", (128, 1024), F32, st) for i in range(3)]
    pden = p.ps("at_pden", (128, 512), F32, st)
    ppv = p.ps("at_ppv", (128, 512), F32, st)

    kvq = p.dq("s5", 8)
    kvt0 = kvq.k
    se0 = p.n("se")
    pe0 = p.n("pe")
    ve0 = p.n("ve")

    sc_marks = {}
    dp_marks = {}
    dpk2_marks = {}
    prd_marks = {}
    exp_idx = {}
    nexp = [0]

    with nc.Block() as blk:
        @blk.tensor
        def _(e):
            def denpv_pair(h, qt, kc2):
                if kc2 == 0:
                    e.wait_ge(p.sem("ve"), ve0 + 2 * (4 * h + qt))
                e.wait_ge(p.sem("se"), se0 + (h * 4 + qt) * 16 + kc2 + 1)
                for half in range(2):
                    kc = 2 * kc2 + half
                    e.matmul(pden[0:1, :], g["ones_col"][:, 0:1],
                             probs[qt % 2][:, kc * 512:(kc + 1) * 512],
                             start=(kc == 0), stop=(kc == 31))
                    mm = e.matmul(ppv[:, :], vh[h % 2][:, kc * 128:(kc + 1) * 128],
                                  probs[qt % 2][:, kc * 512:(kc + 1) * 512],
                                  start=(kc == 0), stop=(kc == 31))
                dpk2_marks[(4 * h + qt, kc2)] = p.inc(mm, "pe", 1)

            def prd(h, qt):
                dp_marks[(h, qt)] = dpk2_marks[(4 * h + qt, 15)]
                e.wait_ge(p.sem("ve"), ve0 + 2 * (4 * h + qt) + 1)
                prd_marks[(h, qt)] = p.inc(
                    e.matmul(pden[:, :], g["ones_row_f32"][0:1, :], rdrow[0:1, :],
                             start=True, stop=True), "pe", 1)

            def score_pair(h, qt, kc2):
                gk2 = 16 * (4 * h + qt) + kc2
                if gk2 >= 3:
                    e.wait_ge(p.sem("se"), se0 + gk2 - 2)
                for half in range(2):
                    kc = 2 * kc2 + half
                    mm = e.matmul(psc[gk2 % 3][:, half * 512:(half + 1) * 512],
                                  kh[h % 2][:, kc * 128:(kc + 1) * 128],
                                  attnout[:, h * 2048 + qt * 512:h * 2048 + (qt + 1) * 512],
                                  start=True, stop=True)
                sc_marks[(h, qt, kc2)] = p.inc(mm, "pe", 1)

            for h in range(16):
                for t4 in range(4):
                    kvq.wait(e, kvt0 + 4 * h + t4 + 1)
                for kc2 in range(16):
                    score_pair(h, 0, kc2)
                for qt in range(1, 4):
                    for kc2 in range(16):
                        score_pair(h, qt, kc2)
                        denpv_pair(h, qt - 1, kc2)
                    prd(h, qt - 1)
                for kc2 in range(16):
                    denpv_pair(h, 3, kc2)
                prd(h, 3)

        @blk.sync
        def _(e):
            e.wait_ge(p.sem("cc"), p.n("cc"))
            for h in range(16):
                if h >= 2:
                    e.wait_ge(p.sem("pe"), dp_marks[(h - 2, 3)])
                kvq.issue(e, kh[h % 2][:, 0:2048], ag_out[h * 128:(h + 1) * 128, :])
                kvq.issue(e, kh[h % 2][:, 2048:4096], ag_out[4096 + h * 128:4096 + (h + 1) * 128, :])
                kvq.issue(e, vh[h % 2][:, 0:2048].rearrange("p (kc d) -> p kc d", kc=16),
                          ag_out[2048:4096, h * 128:(h + 1) * 128]
                          .rearrange("(kc q) d -> q kc d", q=128))
                kvq.issue(e, vh[h % 2][:, 2048:4096].rearrange("p (kc d) -> p kc d", kc=16),
                          ag_out[4096 + 2048:4096 + 4096, h * 128:(h + 1) * 128]
                          .rearrange("(kc q) d -> q kc d", q=128))

        @blk.scalar
        def _(e):
            for h in range(16):
                for qt in range(4):
                    for kc2 in range(16):
                        e.wait_ge(p.sem("pe"), sc_marks[(h, qt, kc2)])
                        gq = 4 * h + qt
                        gq2 = 16 * gq + kc2
                        if gq >= 1:
                            e.wait_ge(p.sem("pe"), dpk2_marks[(gq - 1, kc2)])
                        p.inc(e.activation(probs[qt % 2][:, kc2 * 1024:(kc2 + 1) * 1024],
                                           psc[gq2 % 3][:, :], AF.Exp,
                                           scale=g["qscale_col"][:, 0:1]), "se", 1)

        @blk.vector
        def _(e):
            for h in range(16):
                for qt in range(4):
                    e.wait_ge(p.sem("pe"), dp_marks[(h, qt)])
                    p.inc(e.reciprocal(rdrow[0:1, :], pden[0:1, :]), "ve", 1)
                    e.wait_ge(p.sem("pe"), prd_marks[(h, qt)])
                    r = e.tensor_copy(rdb[:, :], pden[:, :])
                    p.edge(e, r, "vch")
                    p.inc(e.tensor_mul(attnout[:, h * 2048 + qt * 512:h * 2048 + (qt + 1) * 512],
                                       ppv[:, :], rdb[:, :]), "ve", 1)
    st.close()
    p.barrier()



def emit_l1final(p, g, attnout, g1_dram, x1_dram, out_dram):
    nc = p.nc
    st = ExitStack()
    wp = [p.sb(f"fn_wp{i}", (128, NDC * 128), BF16, st) for i in range(2)]
    bpr = p.sb("fn_bpr", (1, 2048), BF16, st)
    s1f = p.sb("fn_s1f", (128, 512), BF16, st)
    s1tok = [p.sb(f"fn_s1t{i}", (128, 128), BF16, st) for i in range(2)]
    g1t = [p.sb(f"fn_g1t{i}", (128, 128), BF16, st) for i in range(4)]
    x1t = [p.sb(f"fn_x1t{i}", (128, 128), F32, st) for i in range(4)]
    x2t = [p.sb(f"fn_x2t{i}", (128, 128), F32, st) for i in range(2)]
    sc = p.sb("fn_sc", (128, 128), F32, st)

    wpq = p.dq("g1", 8)
    ldq = p.dq("s4", 8)
    o2q = p.dq("s3", 4)
    cq = p.dq("g2", 4)
    with nc.Block() as blk:
        @blk.gpsimd
        def _(e):
            cq.issue(e, bpr[:, :], g["d_bp_rows"][:, :])
            cq.wait_all(e)
    p.barrier()

    for oc in range(16):
        stO = ExitStack()
        ps1 = [p.ps(f"fn_ps1{oc}_{i}", (128, 512), F32, stO) for i in range(2)]
        pxT = [p.ps(f"fn_pxT{oc}_{i}", (128, 128), BF16, stO) for i in range(2)]
        se0 = p.n("se")
        pe0 = p.n("pe")
        ve0 = p.n("ve")
        wt0 = wpq.k
        lt0 = ldq.k
        ot0 = o2q.k
        with nc.Block() as blk:
            @blk.gpsimd
            def _(e):
                wpq.issue(e, wp[oc % 2][:, :].rearrange("p (hc o) -> p hc o", hc=NDC),
                          g["d_wp_sw"][oc].rearrange("hc p o -> p hc o"))

            @blk.tensor
            def _(e):
                wpq.wait(e, wt0 + 1)
                for ts in range(4):
                    if ts >= 2:
                        e.wait_ge(p.sem("se"), se0 + ts - 1)
                    for hc in range(NDC):
                        e.matmul(ps1[ts % 2][:, :], wp[oc % 2][:, hc * 128:(hc + 1) * 128],
                                 attnout[:, hc * 2048 + ts * 512:hc * 2048 + (ts + 1) * 512],
                                 start=(hc == 0), stop=False)
                    mm = e.matmul(ps1[ts % 2][:, :], bpr[0:1, oc * 128:(oc + 1) * 128], g["ones_bf"][0:1, :],
                                  start=False, stop=True)
                    p.inc(mm, "pe", 1)
                    for t4 in range(4):
                        k4 = ts * 4 + t4
                        e.wait_ge(p.sem("se"), se0 + ts + 1)
                        if k4 >= 2:
                            e.wait_ge(p.sem("ve"), ve0 + 3 * (k4 - 1))
                        p.inc(e.transpose(pxT[k4 % 2][:, :], s1f[:, t4 * 128:(t4 + 1) * 128],
                                          g["ident_bf"][:, :]), "pe", 1)

            @blk.scalar
            def _(e):
                for ts in range(4):
                    e.wait_ge(p.sem("pe"), pe0 + 5 * ts + 1)
                    if ts >= 1:
                        e.wait_ge(p.sem("pe"), pe0 + 5 * ts)
                    p.inc(e.activation(s1f[:, :], ps1[ts % 2][:, :], AF.Copy), "se", 1)

            @blk.sync
            def _(e):
                for tc in range(4):
                    ldq.issue(e, g1t[tc][:, :], g1_dram[tc * 128:(tc + 1) * 128, oc * 128:(oc + 1) * 128])
                    ldq.issue(e, x1t[tc][:, :], x1_dram[tc * 128:(tc + 1) * 128, oc * 128:(oc + 1) * 128])
                for tc in range(NTC):
                    e.wait_ge(p.sem("ve"), ve0 + 3 * (tc + 1))
                    o2q.issue(e, out_dram[tc * 128:(tc + 1) * 128, oc * 128:(oc + 1) * 128],
                              x2t[tc % 2][:, :])
                    if tc + 4 < NTC:
                        ldq.issue(e, g1t[tc % 4][:, :],
                                  g1_dram[(tc + 4) * 128:(tc + 5) * 128, oc * 128:(oc + 1) * 128])
                        ldq.issue(e, x1t[tc % 4][:, :],
                                  x1_dram[(tc + 4) * 128:(tc + 5) * 128, oc * 128:(oc + 1) * 128])

            @blk.vector
            def _(e):
                for tc in range(NTC):
                    k4 = tc
                    e.wait_ge(p.sem("pe"), pe0 + (tc // 4) * 5 + 1 + (tc % 4) + 1)
                    r = e.tensor_copy(s1tok[k4 % 2][:, :], pxT[k4 % 2][:, :])
                    p.edge(e, r, "vch")
                    ldq.wait(e, lt0 + 2 * tc + 1)
                    ldq.wait(e, lt0 + 2 * tc + 2)
                    r = e.tensor_mul(sc[:, :], s1tok[k4 % 2][:, :], g1t[tc % 4][:, :])
                    p.edge(e, r, "vch")
                    if ot0 + tc - 1 >= 1:
                        o2q.wait(e, ot0 + tc - 1)
                    r = e.tensor_add(x2t[tc % 2][:, :], sc[:, :], x1t[tc % 4][:, :])
                    p.inc(r, "ve", 3)
        stO.close()
        p.barrier()
    with nc.Block() as blk:
        @blk.sync
        def _(e):
            o2q.wait_all(e)
    p.barrier()
    st.close()



def build_full():
    p = Prog()
    g = {}
    for nm, shape, dt in [
        ("d_ident", (128, 128), BF16), ("d_ident_f32", (128, 128), F32),
        ("d_WkT16", (16, 128, 512), F32), ("d_WvT16", (16, 128, 512), F32),
        ("d_wmb_hsw", (16, 16, 128, 128), F32), ("d_bk_row", (1, 512), F32),
        ("d_bv_row", (1, 512), F32), ("d_bmb_row", (1, 2048), F32),
        ("d_mem_aug", (16, 768, 129), F32),
        ("d_wg_sw", (64, 128, 2048), I8), ("d_wu_sw", (64, 128, 2048), I8),
        ("d_wd_sw", (2, 64, 128, 1024), I8), ("d_sg_sw", (128, 64), F32),
        ("d_su_sw", (128, 64), F32), ("d_sdown_bc", (128, 2048), F32),
        ("d_wq_hsw", (16, 16, 128, 128), F32), ("d_wg_gsw", (4, 16, 128, 512), F32),
        ("d_bq_cols", (128, 16), F32), ("d_bg_row", (1, 2048), F32),
        ("d_wk1_hsw", (16, 16, 128, 128), F32), ("d_wv1_sw", (16, 128, 2048), F32),
        ("d_bk1_rows", (1, 2048), F32), ("d_bv1_row", (1, 2048), F32),
        ("d_wg1_sw", (44, 128, 2048), I32), ("d_wu1_sw", (44, 128, 2048), I32),
        ("d_wd1_sw", (4, 44, 128, 512), I32), ("d_sg1_sw", (128, 44), F32),
        ("d_su1_sw", (128, 44), F32), ("d_sd1_bc", (128, 2048), F32),
        ("d_wq1_hsw", (16, 16, 128, 128), F32), ("d_bq1_rows", (1, 2048), F32),
        ("d_wp_sw", (16, 16, 128, 128), F32), ("d_bp_rows", (1, 2048), F32),
    ]:
        g[nm] = p.dram(nm, shape, dt, kind="ExternalInput").ap()
    d_x0 = p.dram("d_x0", (T, DM), F32, kind="ExternalInput").ap()
    o_out = p.dram("o_out", (T, DM), F32, kind="ExternalOutput").ap()

    ar_in = p.dram("ar_in", (12384, 128), F32).ap()
    ar_out = p.dram("ar_out", (12384, 128), F32).ap()
    g0scr = p.dram("g0scr", (T, DM), BF16).ap()
    x2d = p.dram("x2d", (512, DM), BF16).ap()
    x1scr = p.dram("x1scr", (T, DM), F32).ap()
    g1scr = p.dram("g1scr", (T, DM), BF16).ap()
    ag_in = p.dram("ag_in", (4096, 2048), BF16).ap()
    ag_out = p.dram("ag_out", (8192, 2048), BF16).ap()

    xn_stk = ExitStack()
    xnT = xn_stk.enter_context(p.nc.sbuf_tensor("xnT", [128, 16 * T], BF16))
    emit_const_setup(p, g)
    emit_ln(p, d_x0, xnT, g, "ln0")
    emit_l0p1(p, xnT, g, ar_in, ar_out)
    emit_ar(p, ar_in, ar_out)
    emit_ffn0(p, xnT, g, g0scr)
    nm_st = ExitStack()
    nm_aug = nm_st.enter_context(p.nc.sbuf_tensor("nm_aug", [128, 96 * 129], BF16))
    emit_newmem(p, g, ar_out, nm_aug)
    emit_l0p2(p, xnT, g, nm_aug, x2d, g0scr, d_x0, x1scr)
    nm_st.close()
    emit_ln(p, x1scr, xnT, g, "ln1")
    emit_l1kv(p, xnT, g, ag_in, ag_out)
    emit_ffn1(p, xnT, g, g1scr)
    at_st = ExitStack()
    attnout = at_st.enter_context(p.nc.sbuf_tensor("attnout", [128, 16 * T], BF16))
    emit_l1q(p, xnT, g, attnout)
    emit_l1attn(p, g, ag_out, attnout)
    emit_l1final(p, g, attnout, g1scr, x1scr, o_out)
    at_st.close()
    return p.finish()


def make_inmaps(h):
    maps = []
    for c in range(NCORES):
        maps.append({
            "d_ident": h["ident"], "d_ident_f32": h["ident_f32"], "d_x0": h["x0_cores"][c],
            "d_WkT16": h["WkT16"], "d_WvT16": h["WvT16"], "d_wmb_hsw": h["wmb_hsw"],
            "d_bk_row": h["bk_row"], "d_bv_row": h["bv_row"], "d_bmb_row": h["bmb_row"],
            "d_mem_aug": h["mem_aug"],
            "d_wg_sw": h["wg0_sw"], "d_wu_sw": h["wu0_sw"], "d_wd_sw": h["wd0_sw"],
            "d_sg_sw": h["sg0_sw"], "d_su_sw": h["su0_sw"], "d_sdown_bc": h["sd0_bc"],
            "d_wq_hsw": h["wq_hsw"], "d_wg_gsw": h["wg_gsw"],
            "d_bq_cols": h["bq_cols"], "d_bg_row": h["bg_row"],
            "d_wk1_hsw": h["wk1_hsw"], "d_wv1_sw": h["wv1_sw"],
            "d_bk1_rows": h["bk1_rows"], "d_bv1_row": h["bv1_row"],
            "d_wg1_sw": h["wg1_sw"], "d_wu1_sw": h["wu1_sw"], "d_wd1_sw": h["wd1_sw"],
            "d_sg1_sw": h["sg1_sw"], "d_su1_sw": h["su1_sw"], "d_sd1_bc": h["sd1_bc"],
            "d_wq1_hsw": h["wq1_hsw"], "d_bq1_rows": h["bq1_rows"],
            "d_wp_sw": h["wp_sw"], "d_bp_rows": h["bp_rows"],
        })
    return maps


def kernel(**inputs):
    h = hostprep(inputs)
    nc = build_full()
    res = run_bass_kernel_spmd(nc, make_inmaps(h), core_ids=list(range(NCORES)))
    out = np.empty((4, 4096, 2048), dtype=np.float32)
    for c in range(NCORES):
        out[c // 2, (c % 2) * T:(c % 2 + 1) * T] = np.asarray(res.results[c]["o_out"])
    return out
